# version 04 (final)
# baseline (speedup 1.0000x reference)
import os
import contextlib
import numpy as np
import ml_dtypes
import concourse.bass as bass
import concourse.mybir as mybir
from concourse.bass_utils import run_bass_kernel_spmd

F32 = mybir.dt.float32
BF16 = mybir.dt.bfloat16
I32 = mybir.dt.int32
AF = mybir.ActivationFunctionType
ALU = mybir.AluOpType
AX = mybir.AxisListType.X

NCORE = 8
D_MODEL = 2048
KCH = 16
T = 1024
TH = 1030
CS = 256
NCHUNK = 4
H = 64
PHD = 64
NST = 128
G = 8
INTER = 4096
CONV = 6144
PROJ = 10304
FFN_I = 8192
EPS_LN = 1e-5
EPS_RMS = 1e-6
NEG_BIG = -1e30
TAU = 1.0 / np.sqrt(128.0)

GROUPS_TH = [(0, 512), (512, 512), (1024, 6)]
GROUPS_T = [(0, 512), (512, 512)]
GROUPS_REAL_TILE = [(3, 512), (518, 512)]


def f32c(x):
    return np.ascontiguousarray(np.asarray(x), dtype=np.float32)


def i32c(x):
    return np.ascontiguousarray(np.asarray(x), dtype=np.int32)


def col_layout(v, p=128):
    v = np.asarray(v)
    n = v.shape[0] // p
    return np.ascontiguousarray(v.reshape(n, p).T)


def host_prep(inputs):
    sh = {}
    x = np.asarray(inputs["L0_attn_input_states"])
    sh["w_inprojT"] = f32c(np.asarray(inputs["L0_attn_in_proj_w"]).T)
    cw = np.asarray(inputs["L0_attn_conv_w"]).reshape(CONV, 4)
    sh["conv_w"] = f32c(cw.reshape(48, 128, 4).transpose(1, 0, 2).reshape(128, 192))
    sh["conv_b"] = f32c(col_layout(np.asarray(inputs["L0_attn_conv_b"])))
    sh["dt_bias"] = f32c(np.asarray(inputs["L0_attn_dt_bias"]).reshape(H, 1))
    sh["A_log"] = f32c(np.asarray(inputs["L0_attn_A_log"]).reshape(H, 1))
    sh["D_row"] = f32c(np.asarray(inputs["L0_attn_D"]).reshape(1, H))
    sh["norm_w"] = f32c(col_layout(np.asarray(inputs["L0_attn_norm_w"])))
    sh["w_outprojT"] = f32c(np.asarray(inputs["L0_attn_out_proj_w"]).T)
    sh["wgT"] = i32c(np.asarray(inputs["L0_ffn_gate_w"]).T)
    sh["wuT"] = i32c(np.asarray(inputs["L0_ffn_up_w"]).T)
    sh["wdT"] = i32c(np.asarray(inputs["L0_ffn_down_w"]).T)
    sh["gate_scale"] = f32c(col_layout(np.asarray(inputs["L0_ffn_gate_scale"])))
    sh["up_scale"] = f32c(col_layout(np.asarray(inputs["L0_ffn_up_scale"])))
    sh["down_scale"] = f32c(col_layout(np.asarray(inputs["L0_ffn_down_scale"])))
    for nm in ("wq", "wk", "wv", "wo"):
        sh[nm + "T"] = f32c(np.asarray(inputs[f"L1_attn_{nm}"]).T)
    sh["bq_col"] = f32c(col_layout(np.asarray(inputs["L1_attn_bq"])))
    sh["bo_col"] = f32c(col_layout(np.asarray(inputs["L1_attn_bo"])))
    sh["bk_row"] = f32c(np.asarray(inputs["L1_attn_bk"]).reshape(1, D_MODEL))
    sh["bv_row"] = f32c(np.asarray(inputs["L1_attn_bv"]).reshape(1, D_MODEL))
    sh["wqkvT"] = f32c(np.asarray(inputs["L1_ffn_W_qkv"]).T)
    sh["wgateT"] = f32c(np.asarray(inputs["L1_ffn_W_gate"]).T)

    per = []
    for core in range(NCORE):
        b, c = core // 4, core % 4
        blkA, blkB = c, 7 - c
        xb = x[b]
        cols = []
        if blkA == 0:
            cols.append(np.zeros((3, D_MODEL), np.float32))
        else:
            cols.append(xb[512 * blkA - 3:512 * blkA])
        cols.append(xb[512 * blkA:512 * blkA + 512])
        cols.append(xb[512 * blkB - 3:512 * blkB])
        cols.append(xb[512 * blkB:512 * blkB + 512])
        xt = f32c(np.concatenate(cols, axis=0).T)
        flags = np.full((1, 40), NEG_BIG, np.float32)
        flags[0, :4 * blkA] = 0.0
        flags[0, 12:12 + 4 * blkB] = 0.0
        per.append(dict(x_t=xt, flags=flags))
    return sh, per


def gather_row_of_block(gblk):
    r = gblk if gblk <= 3 else 7 - gblk
    off = 0 if gblk <= 3 else 512
    return r * 1024 + off


class Sem:
    __slots__ = ("h", "v")

    def __init__(self, h):
        self.h = h
        self.v = 0

    def mark(self, instr, n):
        instr.then_inc(self.h, n)
        self.v += n
        return self.v


class Ctx:

    def __init__(self, nc, stack):
        self.nc = nc
        self.stack = stack
        self._sems = {}
        self.C = {}

    def sem(self, name):
        if name not in self._sems:
            h = self.stack.enter_context(self.nc.semaphore(name))
            self._sems[name] = Sem(h)
        return self._sems[name]

    def sb(self, name, shape, dtype):
        return self.stack.enter_context(self.nc.sbuf_tensor(name, shape, dtype))

    def ps(self, name, shape, dtype=F32):
        return self.stack.enter_context(self.nc.psum_tensor(name, shape, dtype))


def emit_reset_preamble(ctx):
    nc = ctx.nc
    rng = nc._kernel_sem_range
    with nc.Block() as blk:
        @blk.gpsimd
        def _(g):
            g.dma_reset(rng)
    barrier(ctx, "reset")


def emit_consts(ctx):
    nc = ctx.nc
    C = ctx.C
    C["ones_b16"] = ctx.sb("c_ones_b16", [128, 128], BF16)
    C["m128_b16"] = ctx.sb("c_m128_b16", [128, 128], BF16)
    C["ident_b16"] = ctx.sb("c_ident_b16", [128, 128], BF16)
    C["ident_f32"] = ctx.sb("c_ident_f32", [128, 128], F32)
    C["tri_f32"] = ctx.sb("c_tri_f32", [128, 128], F32)
    C["onesrow_f32"] = ctx.sb("c_onesrow_f32", [1, 128], F32)
    C["onesrow_b16"] = ctx.sb("c_onesrow_b16", [1, 128], BF16)
    C["onescol_b16"] = ctx.sb("c_onescol_b16", [128, 1], BF16)
    C["sel4"] = ctx.sb("c_sel4", [4, 512], F32)
    C["eps_ln"] = ctx.sb("c_eps_ln", [128, 1], F32)
    C["eps_rms"] = ctx.sb("c_eps_rms", [128, 1], F32)
    C["e8"] = ctx.sb("c_e8", [128, 1], F32)
    with nc.Block() as blk:
        @blk.gpsimd
        def _(g):
            g.memset(C["ones_b16"][:, :], 1.0)
            g.memset(C["m128_b16"][:, :], 128.0)
            g.memset(C["onesrow_f32"][:, :], 1.0)
            g.memset(C["onesrow_b16"][:, :], 1.0)
            g.memset(C["onescol_b16"][:, :], 1.0)
            g.memset(C["eps_ln"][:, :], EPS_LN)
            g.memset(C["eps_rms"][:, :], EPS_RMS)
            g.memset(C["e8"][:, :], 1e-8)
            g.memset(C["ident_b16"][:, :], 0.0)
            g.affine_select(out=C["ident_b16"][:, :], in_=C["ident_b16"][:, :],
                            compare_op=ALU.not_equal, fill=1.0, base=0,
                            pattern=[[-1, 128]], channel_multiplier=1)
            g.memset(C["ident_f32"][:, :], 0.0)
            g.affine_select(out=C["ident_f32"][:, :], in_=C["ident_f32"][:, :],
                            compare_op=ALU.not_equal, fill=1.0, base=0,
                            pattern=[[-1, 128]], channel_multiplier=1)
            g.memset(C["sel4"][:, :], 0.0)
            for q in range(4):
                g.affine_select(out=C["sel4"][:, q * 128:(q + 1) * 128],
                                in_=C["sel4"][:, q * 128:(q + 1) * 128],
                                compare_op=ALU.not_equal, fill=1.0, base=-q,
                                pattern=[[0, 128]], channel_multiplier=1)
            g.memset(C["tri_f32"][:, :], 0.0)
            g.affine_select(out=C["tri_f32"][:, :], in_=C["tri_f32"][:, :],
                            compare_op=ALU.is_ge, fill=NEG_BIG, base=0,
                            pattern=[[1, 128]], channel_multiplier=-1)
    barrier(ctx, "consts")


def barrier(ctx, tag):
    nc = ctx.nc
    s = ctx.sem("bar")
    target = s.v + 5
    with nc.Block() as blk:
        for engname in ("tensor", "vector", "scalar", "gpsimd", "sync"):
            def mk(e):
                e.nop().then_inc(s.h, 1)
                e.wait_ge(s.h, target)
            getattr(blk, engname)(mk)
    s.v = target


def emit_p1(ctx, x_t, xn_out):
    nc = ctx.nc
    ones_b = ctx.C["ones_b16"]
    _st = contextlib.ExitStack()
    _sb = lambda n, s, d: _st.enter_context(nc.sbuf_tensor(n, s, d))
    _ps = lambda n, s: _st.enter_context(nc.psum_tensor(n, s, F32))
    xf = _sb("p1_xf", [128, KCH * TH], F32)
    xb = _sb("p1_xb", [128, KCH * TH], BF16)
    sq = _sb("p1_sq", [128, 2 * TH], BF16)
    mrep = _sb("p1_m", [128, TH], F32)
    vtmp = _sb("p1_v", [128, TH], F32)
    rrep = _sb("p1_r", [128, TH], F32)
    tmp = _sb("p1_t", [128, TH], F32)
    xnb = _sb("p1_xnb", [128, 2 * TH], BF16)
    ps_s = _ps("p1_ps_s", [128, TH])
    ps_q = _ps("p1_ps_q", [128, TH])
    ctx.dbg_p1 = (mrep, vtmp, rrep)

    ld = ctx.sem("p1_ld")
    lb = ctx.sem("p1_lb")
    sqs = ctx.sem("p1_sq")
    mms = ctx.sem("p1_mm")
    nrm = ctx.sem("p1_nrm")
    stp = [ctx.sem("p1_st0"), ctx.sem("p1_st1")]

    ld0, lb0, sq0, mm0, nrm0 = ld.v, lb.v, sqs.v, mms.v, nrm.v
    st_base = [stp[0].v, stp[1].v]
    LD_ALL = ld0 + 16 * KCH
    LB_ALL = lb0 + 16 * KCH
    M_PSUM, M_MEAN, M_VAR, M_RSTD = mm0 + 18, mm0 + 19, mm0 + 20, mm0 + 21

    with nc.Block() as blk:
        @blk.sync
        def _(sy):
            for k in range(KCH):
                sy.dma_start(out=xf[:, k * TH:(k + 1) * TH],
                             in_=x_t[k * 128:(k + 1) * 128, :]).then_inc(ld.h, 16)
            for k in range(KCH):
                sy.wait_ge(nrm.h, nrm0 + 1 + k)
                sy.dma_start(out=xn_out[k * 128:(k + 1) * 128, :],
                             in_=xnb[:, (k % 2) * TH:(k % 2 + 1) * TH]
                             ).then_inc(stp[k % 2].h, 16)
            sy.wait_ge(stp[0].h, st_base[0] + 16 * 8)
            sy.wait_ge(stp[1].h, st_base[1] + 16 * 8)

        @blk.gpsimd
        def _(g):
            for k in range(KCH):
                g.dma_start(out=xb[:, k * TH:(k + 1) * TH],
                            in_=x_t[k * 128:(k + 1) * 128, :]).then_inc(lb.h, 16)

        @blk.scalar
        def _(s):
            s.wait_ge(lb.h, LB_ALL)
            for k in range(KCH):
                if k >= 2:
                    s.wait_ge(mms.h, mm0 + (k - 1))
                s.activation(sq[:, (k % 2) * TH:(k % 2 + 1) * TH],
                             xb[:, k * TH:(k + 1) * TH], AF.Square).then_inc(sqs.h, 1)
            s.wait_ge(mms.h, M_PSUM)
            s.activation(mrep[:, :], ps_s[:, :], AF.Copy, scale=1.0 / D_MODEL)
            s.activation(tmp[:, :], mrep[:, :], AF.Square).then_inc(mms.h, 1)
            s.wait_ge(mms.h, M_VAR)
            s.activation(rrep[:, :], vtmp[:, :], AF.Ln, bias=ctx.C["eps_ln"][:, 0:1])
            s.activation(rrep[:, :], rrep[:, :], AF.Exp, scale=-0.5).then_inc(mms.h, 1)

        @blk.tensor
        def _(t):
            t.wait_ge(lb.h, LB_ALL)
            for k in range(KCH):
                for (g0, gw) in GROUPS_TH:
                    t.matmul(ps_s[:, g0:g0 + gw], ones_b[:, :],
                             xb[:, k * TH + g0:k * TH + g0 + gw],
                             start=(k == 0), stop=(k == KCH - 1))
                t.wait_ge(sqs.h, sq0 + k + 1)
                last = None
                for (g0, gw) in GROUPS_TH:
                    last = t.matmul(ps_q[:, g0:g0 + gw], ones_b[:, :],
                                    sq[:, (k % 2) * TH + g0:(k % 2) * TH + g0 + gw],
                                    start=(k == 0), stop=(k == KCH - 1))
                last.then_inc(mms.h, 3 if k == KCH - 1 else 1)

        @blk.vector
        def _(v):
            v.wait_ge(mms.h, M_MEAN)
            v.scalar_tensor_tensor(vtmp[:, :], ps_q[:, :], 1.0 / D_MODEL, tmp[:, :],
                                   ALU.mult, ALU.subtract).then_inc(mms.h, 1)
            v.wait_ge(mms.h, M_RSTD)
            v.wait_ge(ld.h, LD_ALL)
            for k in range(KCH):
                if k >= 2:
                    v.wait_ge(stp[k % 2].h, st_base[k % 2] + 16 * (k // 2))
                v.tensor_sub(tmp[:, :], xf[:, k * TH:(k + 1) * TH], mrep[:, :])
                v.tensor_mul(xnb[:, (k % 2) * TH:(k % 2 + 1) * TH],
                             tmp[:, :], rrep[:, :]).then_inc(nrm.h, 1)

    _st.close()
    return


def np_layernorm(x, axis=-1, eps=EPS_LN):
    m = x.mean(axis=axis, keepdims=True)
    v = x.var(axis=axis, keepdims=True)
    return (x - m) / np.sqrt(v + eps)


def np_silu(x):
    return x / (1.0 + np.exp(-x))


def np_softplus(x):
    return np.logaddexp(0.0, x)

def emit_p2(ctx, xn_in, x_t, wgT, wuT, wdT, gsc, usc, dsc, h_buf, xffn_out):
    nc = ctx.nc
    _p2st = contextlib.ExitStack()
    Sh_sb = _p2st.enter_context(nc.sbuf_tensor("p2_Sh", [128, T], F32))
    if os.environ.get("P2B_ONLY"):
        _emit_p2b_only_shim(ctx, Sh_sb, h_buf)
        if not os.environ.get("P2B_SKIP"):
            _emit_p2b(ctx, x_t, wdT, dsc, h_buf, xffn_out, Sh_sb)
        return
    with contextlib.ExitStack() as stA:
        sbA = lambda n, s, d: stA.enter_context(nc.sbuf_tensor(n, s, d))
        psA = lambda n, s: stA.enter_context(nc.psum_tensor(n, s, F32))
        xn_sb = sbA("p2_xn", [128, KCH * TH], BF16)
        gs_sb = sbA("p2_gs", [128, 64], F32)
        us_sb = sbA("p2_us", [128, 64], F32)
        Srep = sbA("p2_Srep", [128, T], F32)
        wg_sb = sbA("p2_wg", [128, 2 * 4096], BF16)
        wu_sb = sbA("p2_wu", [128, 2 * 4096], BF16)
        dg_sb = sbA("p2_dg", [128, 2 * 512], F32)
        sg_sb = sbA("p2_sg", [128, 2 * 512], F32)
        du_sb = sbA("p2_du", [128, 2 * 512], F32)
        h_ring = sbA("p2_h", [128, 8 * T], BF16)
        ps_S = psA("p2_ps_S", [128, T])
        ps_sh = psA("p2_ps_sh", [128, T])
        ps_g = [psA("p2_ps_g0", [128, 512]), psA("p2_ps_g1", [128, 512])]
        ps_u = [psA("p2_ps_u0", [128, 512]), psA("p2_ps_u1", [128, 512])]

        ld = ctx.sem("p2_ld")
        wl = [ctx.sem("p2_wl0"), ctx.sem("p2_wl1")]
        wf = [ctx.sem("p2_wf0"), ctx.sem("p2_wf1")]
        mmd = ctx.sem("p2_mmd")
        pfr = ctx.sem("p2_pfr")
        vg = ctx.sem("p2_vg")
        sg = ctx.sem("p2_sg")
        hrd = ctx.sem("p2_hrd")
        shd = ctx.sem("p2_shd")
        hst = [ctx.sem("p2_hst0"), ctx.sem("p2_hst1")]
        ssd = ctx.sem("p2_ssd")

        ld0 = ld.v
        wl0 = [wl[0].v, wl[1].v]
        wf0 = [wf[0].v, wf[1].v]
        mmd0, pfr0, vg0, sg0, hrd0, shd0, ssd0 = mmd.v, pfr.v, vg.v, sg.v, hrd.v, shd.v, ssd.v
        hst0 = [hst[0].v, hst[1].v]
        LD_ALL = ld0 + 16 * (KCH + 2)
        NOG = 32
        NIT = 128
        M_PSS, M_SREP, M_PSH, M_SHREP = ssd0 + 1, ssd0 + 2, ssd0 + 3, ssd0 + 4

        with nc.Block() as blk:
            @blk.sync
            def _(sy):
                for k in range(KCH):
                    sy.dma_start(out=xn_sb[:, k * TH:(k + 1) * TH],
                                 in_=xn_in[k * 128:(k + 1) * 128, :]).then_inc(ld.h, 16)
                sy.dma_start(out=gs_sb[:, :], in_=gsc).then_inc(ld.h, 16)
                sy.dma_start(out=us_sb[:, :], in_=usc).then_inc(ld.h, 16)
                for g4 in range(16):
                    sy.wait_ge(hrd.h, hrd0 + 8 * (g4 + 1))
                    src = h_ring[:, (g4 % 2) * 4 * T:((g4 % 2) + 1) * 4 * T]
                    sy.dma_start(
                        out=h_buf[g4 * 512:(g4 + 1) * 512, :].rearrange(
                            "(o p) t -> p o t", p=128),
                        in_=src.rearrange("p (o t) -> p o t", t=T),
                    ).then_inc(hst[g4 % 2].h, 16)
                sy.wait_ge(hst[0].h, hst0[0] + 16 * 8)
                sy.wait_ge(hst[1].h, hst0[1] + 16 * 8)

            @blk.gpsimd
            def _(g):
                for og in range(NOG):
                    par = og % 2
                    if og >= 2:
                        g.wait_ge(wf[par].h, wf0[par] + og // 2)
                    g.dma_start(
                        out=wg_sb[:, par * 4096:(par + 1) * 4096].rearrange(
                            "p (k j) -> p k j", j=256),
                        in_=wgT[:, og * 256:(og + 1) * 256].rearrange(
                            "(k p) j -> p k j", p=128),
                    ).then_inc(wl[par].h, 16)
                    g.dma_start(
                        out=wu_sb[:, par * 4096:(par + 1) * 4096].rearrange(
                            "p (k j) -> p k j", j=256),
                        in_=wuT[:, og * 256:(og + 1) * 256].rearrange(
                            "(k p) j -> p k j", p=128),
                    ).then_inc(wl[par].h, 16)

            @blk.tensor
            def _(t):
                t.wait_ge(ld.h, LD_ALL)
                last = None
                for k in range(KCH):
                    for hf, (g0, gw) in enumerate(GROUPS_REAL_TILE):
                        last = t.matmul(ps_S[:, hf * 512:hf * 512 + 512],
                                        ctx.C["m128_b16"][:, :],
                                        xn_sb[:, k * TH + g0:k * TH + g0 + gw],
                                        start=(k == 0), stop=(k == KCH - 1))
                last.then_inc(ssd.h, 1)
                for og in range(NOG):
                    par = og % 2
                    t.wait_ge(wl[par].h, wl0[par] + 32 * (og // 2 + 1))
                    lastw = None
                    for ot in range(2):
                        o = og * 2 + ot
                        for hf in range(2):
                            it = o * 2 + hf
                            ppar = it % 2
                            if it >= 2:
                                t.wait_ge(pfr.h, pfr0 + it - 1)
                            g0, gw = GROUPS_REAL_TILE[hf]
                            for k in range(KCH):
                                t.matmul(ps_g[ppar][:, :],
                                         wg_sb[:, par * 4096 + k * 256 + ot * 128:
                                               par * 4096 + k * 256 + ot * 128 + 128],
                                         xn_sb[:, k * TH + g0:k * TH + g0 + gw],
                                         start=(k == 0), stop=(k == KCH - 1))
                            for k in range(KCH):
                                lastw = t.matmul(
                                    ps_u[ppar][:, :],
                                    wu_sb[:, par * 4096 + k * 256 + ot * 128:
                                          par * 4096 + k * 256 + ot * 128 + 128],
                                    xn_sb[:, k * TH + g0:k * TH + g0 + gw],
                                    start=(k == 0), stop=(k == KCH - 1))
                            lastw.then_inc(mmd.h, 1)
                            if it >= 2:
                                itp = it - 2
                                op, hp = itp // 2, itp % 2
                                t.wait_ge(hrd.h, hrd0 + itp + 1)
                                t.matmul(ps_sh[:, hp * 512:hp * 512 + 512],
                                         ctx.C["m128_b16"][:, :],
                                         h_ring[:, (op % 8) * T + hp * 512:
                                                (op % 8) * T + hp * 512 + 512],
                                         start=(itp <= 1), stop=False,
                                         ).then_inc(shd.h, 1)
                    t.nop().then_inc(wf[par].h, 1)
                for itp in (NIT - 2, NIT - 1):
                    op, hp = itp // 2, itp % 2
                    t.wait_ge(hrd.h, hrd0 + itp + 1)
                    mmx = t.matmul(ps_sh[:, hp * 512:hp * 512 + 512],
                                   ctx.C["m128_b16"][:, :],
                                   h_ring[:, (op % 8) * T + hp * 512:(op % 8) * T + hp * 512 + 512],
                                   start=False, stop=True)
                    if itp == NIT - 1:
                        mmx.then_inc(ssd.h, 1)
                    else:
                        mmx.then_inc(shd.h, 1)

            @blk.scalar
            def _(s):
                s.wait_ge(ssd.h, M_PSS)
                s.activation(Srep[:, :], ps_S[:, :], AF.Copy).then_inc(ssd.h, 1)
                for it in range(NIT):
                    s.wait_ge(vg.h, vg0 + it + 1)
                    o, hf = it // 2, it % 2
                    s.activation(sg_sb[:, (it % 2) * 512:(it % 2) * 512 + 512],
                                 dg_sb[:, (it % 2) * 512:(it % 2) * 512 + 512],
                                 AF.Silu, scale=gs_sb[:, o:o + 1]).then_inc(sg.h, 1)
                s.wait_ge(ssd.h, M_PSH)
                s.activation(Sh_sb[:, :], ps_sh[:, :], AF.Copy).then_inc(ssd.h, 1)

            @blk.vector
            def _(v):
                v.wait_ge(ssd.h, M_SREP)
                for it in range(NIT):
                    o, hf = it // 2, it % 2
                    ppar = it % 2
                    v.wait_ge(mmd.h, mmd0 + it + 1)
                    v.tensor_sub(dg_sb[:, ppar * 512:ppar * 512 + 512],
                                 ps_g[ppar][:, :],
                                 Srep[:, hf * 512:hf * 512 + 512]).then_inc(vg.h, 1)
                    v.tensor_sub(du_sb[:, ppar * 512:ppar * 512 + 512],
                                 ps_u[ppar][:, :],
                                 Srep[:, hf * 512:hf * 512 + 512]).then_inc(pfr.h, 1)
                    if o >= 8 and hf == 0:
                        g4old = (o - 8) // 4
                        v.wait_ge(hst[g4old % 2].h,
                                  hst0[g4old % 2] + 16 * (g4old // 2 + 1))
                    if it >= 16:
                        v.wait_ge(shd.h, shd0 + it - 15)
                    v.wait_ge(sg.h, sg0 + it + 1)
                    v.scalar_tensor_tensor(
                        h_ring[:, (o % 8) * T + hf * 512:(o % 8) * T + hf * 512 + 512],
                        du_sb[:, ppar * 512:ppar * 512 + 512],
                        us_sb[:, o:o + 1],
                        sg_sb[:, ppar * 512:ppar * 512 + 512],
                        ALU.mult, ALU.mult).then_inc(hrd.h, 1)

    barrier(ctx, "p2ab")
    _emit_p2b(ctx, x_t, wdT, dsc, h_buf, xffn_out, Sh_sb)
    _p2st.close()


def _emit_p2b_only_shim(ctx, Sh_sb, h_buf):
    nc = ctx.nc
    with contextlib.ExitStack() as st:
        hs = st.enter_context(nc.sbuf_tensor("shim_h", [128, T], BF16))
        pss = st.enter_context(nc.psum_tensor("shim_ps", [128, T], F32))
        ld = ctx.sem("shim_ld")
        mm = ctx.sem("shim_mm")
        with nc.Block() as blk:
            @blk.gpsimd
            def _(g):
                for k in range(64):
                    g.wait_ge(mm.h, k)
                    g.dma_start(out=hs[:, :], in_=h_buf[k * 128:(k + 1) * 128, :]).then_inc(ld.h, 16)
            @blk.tensor
            def _(t):
                for k in range(64):
                    t.wait_ge(ld.h, 16 * (k + 1))
                    for hf in range(2):
                        t.matmul(pss[:, hf * 512:hf * 512 + 512], ctx.C["m128_b16"][:, :],
                                 hs[:, hf * 512:hf * 512 + 512],
                                 start=(k == 0), stop=(k == 63)).then_inc(mm.h, 1) if hf == 1 else \
                        t.matmul(pss[:, hf * 512:hf * 512 + 512], ctx.C["m128_b16"][:, :],
                                 hs[:, hf * 512:hf * 512 + 512],
                                 start=(k == 0), stop=(k == 63))
            @blk.scalar
            def _(s):
                s.wait_ge(mm.h, 64)
                s.activation(Sh_sb[:, :], pss[:, :], AF.Copy).then_inc(mm.h, 1)
            @blk.vector
            def _(v):
                v.wait_ge(mm.h, 65)
    barrier(ctx, "shim")


def _emit_p2b(ctx, x_t, wdT, dsc, h_buf, xffn_out, Sh_sb):
    nc = ctx.nc
    CUT = os.environ.get("P2B_CUT", "")
    with contextlib.ExitStack() as stB:
        sbB = lambda n, s, d: stB.enter_context(nc.sbuf_tensor(n, s, d))
        psB = lambda n, s: stB.enter_context(nc.psum_tensor(n, s, F32))
        wd_sb = sbB("p2_wd", [128, 2 * 8192], BF16)
        h_sb = sbB("p2_hs", [128, 64 * T], BF16)
        ds_sb = sbB("p2_ds", [128, 16], F32)
        xsl = sbB("p2_xsl", [128, 4 * 512], F32)
        t1 = sbB("p2_t1", [128, 512], F32)
        xo = sbB("p2_xo", [128, 4 * 512], F32)
        ps_d = [psB(f"p2_psd{i}", [128, 512]) for i in range(4)]

        ld2 = ctx.sem("p2_ld2")
        wl2 = [ctx.sem("p2_wl2a"), ctx.sem("p2_wl2b")]
        wf2 = [ctx.sem("p2_wf2a"), ctx.sem("p2_wf2b")]
        mm2 = ctx.sem("p2_mm2")
        ev2 = ctx.sem("p2_ev2")
        st2 = ctx.sem("p2_st2")
        xl2 = ctx.sem("p2_xl2")
        ld20, mm20, ev20, st20, xl20 = ld2.v, mm2.v, ev2.v, st2.v, xl2.v
        wl20 = [wl2[0].v, wl2[1].v]
        wf20 = [wf2[0].v, wf2[1].v]
        NPASS = 8

        with nc.Block() as blk:
            @blk.sync
            def _(sy):
                sy.dma_start(out=ds_sb[:, :], in_=dsc).then_inc(ld2.h, 16)
                for p in range(NPASS):
                    s, hf = p // 2, p % 2
                    g0, gw = GROUPS_REAL_TILE[hf]
                    if p >= 2:
                        sy.wait_ge(ev2.h, ev20 + 4 * (p - 1))
                    sy.dma_start(
                        out=xsl[:, :].rearrange("p (o t) -> p o t", t=512),
                        in_=x_t[s * 512:(s + 1) * 512, g0:g0 + gw].rearrange(
                            "(o p) t -> p o t", p=128),
                    ).then_inc(xl2.h, 16)
                    if p >= 1:
                        sy.wait_ge(ev2.h, ev20 + 4 * p)
                        sp, hp = (p - 1) // 2, (p - 1) % 2
                        sy.dma_start(
                            out=xffn_out[sp * 512:(sp + 1) * 512,
                                         hp * 512:hp * 512 + 512].rearrange(
                                "(o p) t -> p o t", p=128),
                            in_=xo[:, :].rearrange("p (o t) -> p o t", t=512),
                        ).then_inc(st2.h, 16)
                sy.wait_ge(ev2.h, ev20 + 4 * NPASS)
                sy.dma_start(
                    out=xffn_out[3 * 512:4 * 512, 512:1024].rearrange(
                        "(o p) t -> p o t", p=128),
                    in_=xo[:, :].rearrange("p (o t) -> p o t", t=512),
                ).then_inc(st2.h, 16)
                sy.wait_ge(st2.h, st20 + 16 * NPASS)

            @blk.gpsimd
            def _(g):
                for hq in range(4):
                    g.dma_start(
                        out=h_sb[:, hq * 16 * T:(hq + 1) * 16 * T].rearrange(
                            "p (k t) -> p k t", t=T),
                        in_=h_buf[hq * 2048:(hq + 1) * 2048, :].rearrange(
                            "(k p) t -> p k t", p=128),
                    ).then_inc(ld2.h, 16)
                for p in range(NPASS):
                    s = p // 2
                    for r in range(4):
                        j = p * 4 + r
                        par = j % 2
                        if j >= 2:
                            g.wait_ge(wf2[par].h, wf20[par] + j // 2)
                        g.dma_start(
                            out=wd_sb[:, par * 8192:(par + 1) * 8192].rearrange(
                                "p (k j) -> p k j", j=512),
                            in_=wdT[r * 2048:(r + 1) * 2048,
                                    s * 512:(s + 1) * 512].rearrange(
                                "(k p) j -> p k j", p=128),
                        ).then_inc(wl2[par].h, 16)

            @blk.tensor
            def _(t):
                t.wait_ge(ld2.h, ld20 + 16 * 5)
                for p in range(NPASS):
                    s, hf = p // 2, p % 2
                    if p >= 1:
                        t.wait_ge(ev2.h, ev20 + 4 * p)
                    for r in range(4):
                        j = p * 4 + r
                        par = j % 2
                        t.wait_ge(wl2[par].h, wl20[par] + 16 * (j // 2 + 1))
                        if "m" not in CUT:
                            for k in range(16):
                                kk = r * 16 + k
                                for oc in range(4):
                                    lastmm = t.matmul(
                                        ps_d[oc][:, :],
                                        wd_sb[:, par * 8192 + k * 512 + oc * 128:
                                              par * 8192 + k * 512 + oc * 128 + 128],
                                        h_sb[:, kk * T + hf * 512:kk * T + hf * 512 + 512],
                                        start=(r == 0 and k == 0),
                                        stop=(r == 3 and k == 15))
                        t.nop().then_inc(wf2[par].h, 1)
                    if "m" not in CUT:
                        lastmm.then_inc(mm2.h, 1)
                    else:
                        t.nop().then_inc(mm2.h, 1)

            @blk.vector
            def _(v):
                v.wait_ge(ld2.h, ld20 + 16)
                for p in range(NPASS):
                    s, hf = p // 2, p % 2
                    v.wait_ge(mm2.h, mm20 + p + 1)
                    if "d" in CUT:
                        for _ in range(64):
                            v.nop()
                    v.wait_ge(xl2.h, xl20 + 16 * (p + 1))
                    if p >= 2:
                        v.wait_ge(st2.h, st20 + 16 * (p - 1))
                    for oc in range(4):
                        if "v" in CUT:
                            v.nop().then_inc(ev2.h, 1)
                        else:
                            v.tensor_sub(t1[:, :], ps_d[oc][:, :],
                                         Sh_sb[:, hf * 512:hf * 512 + 512])
                            v.scalar_tensor_tensor(
                                xo[:, oc * 512:oc * 512 + 512],
                                t1[:, :], ds_sb[:, s * 4 + oc:s * 4 + oc + 1],
                                xsl[:, oc * 512:oc * 512 + 512],
                                ALU.mult, ALU.add).then_inc(ev2.h, 1)


def emit_p3(ctx, xn_in, wipT, convw_in, convb_in, dtb_in, alog_in,
            gate_buf, hidT_buf, B_buf, C_buf, acum_buf, nacT_buf, dt2T_buf,
            UD_buf):
    nc = ctx.nc
    with contextlib.ExitStack() as st:
        sb = lambda n, s, d: st.enter_context(nc.sbuf_tensor(n, s, d))
        ps = lambda n, s, d=F32: st.enter_context(nc.psum_tensor(n, s, d))
        xn_sb = sb("p3_xn", [128, KCH * TH], BF16)
        wsb = sb("p3_w", [128, 2 * 16 * 1024], BF16)
        convw = sb("p3_cw", [128, 192], F32)
        convb = sb("p3_cb", [128, 48], F32)
        dtb = sb("p3_dtb", [64, 1], F32)
        alog = sb("p3_alog", [64, 1], F32)
        cin = sb("p3_cin", [128, 2 * TH], F32)
        acc = sb("p3_acc", [128, 2 * T], F32)
        hh = sb("p3_hh", [128, 2 * T], BF16)
        bc = sb("p3_bc", [128, 2 * T], BF16)
        gr = sb("p3_gr", [128, 2 * T], F32)
        htr = sb("p3_htr", [128, 2 * 8 * 128], BF16)
        dt_sb = sb("p3_dt", [64, T], F32)
        dtr = sb("p3_dtr", [64, T], F32)
        dta = sb("p3_dta", [64, T], F32)
        dt2 = sb("p3_dt2", [64, T], F32)
        A_c = sb("p3_Ac", [64, T], F32)
        A_cum = sb("p3_Acum", [64, T], F32)
        nA_cum = sb("p3_nAcum", [64, T], F32)
        ones64 = sb("p3_ones64", [64, 256], F32)
        expA = sb("p3_expA", [64, 1], F32)
        nexpA = sb("p3_nexpA", [64, 1], F32)
        nacT = sb("p3_nacT", [128, 8 * 64], F32)
        dt2T = sb("p3_dt2T", [128, 8 * 64], F32)
        alrow = sb("p3_alrow", [1, 4 * 64], F32)
        alrep = sb("p3_alrep", [128, 4 * 64], F32)
        utmp = sb("p3_utmp", [128, 64], F32)
        uexp = sb("p3_uexp", [128, 64], F32)
        UDs = sb("p3_UD", [128, 8 * 64], F32)
        ps_io = [ps("p3_psio0", [128, TH]), ps("p3_psio1", [128, TH])]
        ps_tr = ps("p3_pstr", [128, 128], BF16)
        ps_al = ps("p3_psal", [128, 128], F32)

        lx = ctx.sem("p3_lx")
        wl = [ctx.sem("p3_wl0"), ctx.sem("p3_wl1")]
        wfr = [ctx.sem("p3_wf0"), ctx.sem("p3_wf1")]
        mmio = ctx.sem("p3_mmio")
        ioev = ctx.sem("p3_ioev")
        convd = ctx.sem("p3_convd")
        hhrd = ctx.sem("p3_hhrd")
        trd = ctx.sem("p3_trd")
        trev = ctx.sem("p3_trev")
        dts = ctx.sem("p3_dts")
        hst = [ctx.sem("p3_hst0"), ctx.sem("p3_hst1")]
        bcs = [ctx.sem("p3_bcs0"), ctx.sem("p3_bcs1")]
        gst = [ctx.sem("p3_gst0"), ctx.sem("p3_gst1")]
        sst = ctx.sem("p3_sst")
        lx0, mmio0, ioev0, convd0, hhrd0, trd0, trev0, dts0 = (
            lx.v, mmio.v, ioev.v, convd.v, hhrd.v, trd.v, trev.v, dts.v)
        wl0 = [wl[0].v, wl[1].v]
        wfr0 = [wfr[0].v, wfr[1].v]
        hst0 = [hst[0].v, hst[1].v]
        bcs0 = [bcs[0].v, bcs[1].v]
        gst0 = [gst[0].v, gst[1].v]
        sst0 = sst.v
        LX_ALL = lx0 + 16 * (KCH + 4)

        groups = [(INTER + CONV, 64, [0])]
        for i in range(6):
            groups.append((INTER + i * 1024, 1024, list(range(1 + i * 8, 1 + i * 8 + 8))))
        for i in range(4):
            groups.append((i * 1024, 1024, list(range(49 + i * 8, 49 + i * 8 + 8))))

        def tile_kind(tn):
            if tn == 0:
                return "dt"
            if tn <= 48:
                return "conv"
            return "gate"

        NT = 81

        with nc.Block() as blk:
            @blk.sync
            def _(sy):
                for k in range(KCH):
                    sy.dma_start(out=xn_sb[:, k * TH:(k + 1) * TH],
                                 in_=xn_in[k * 128:(k + 1) * 128, :]).then_inc(lx.h, 16)
                sy.dma_start(out=convw[:, :], in_=convw_in).then_inc(lx.h, 16)
                sy.dma_start(out=convb[:, :], in_=convb_in).then_inc(lx.h, 16)
                sy.dma_start(out=dtb[:, :], in_=dtb_in).then_inc(lx.h, 16)
                sy.dma_start(out=alog[:, :], in_=alog_in).then_inc(lx.h, 16)
                for hc in range(32):
                    sy.wait_ge(trev.h, trev0 + 8 * (hc + 1))
                    sy.dma_start(
                        out=hidT_buf[:, hc * 128:(hc + 1) * 128].rearrange(
                            "(tt p) c -> p tt c", p=128),
                        in_=htr[:, (hc % 2) * 1024:(hc % 2) * 1024 + 1024].rearrange(
                            "p (tt c) -> p tt c", c=128),
                    ).then_inc(hst[hc % 2].h, 16)
                for i in range(16):
                    ch = 32 + i
                    sy.wait_ge(hhrd.h, hhrd0 + 2 * (ch + 1))
                    dst = (B_buf if i < 8 else C_buf)
                    gg = i % 8
                    sy.dma_start(
                        out=dst[gg * 128:(gg + 1) * 128, :],
                        in_=bc[:, (i % 2) * T:(i % 2) * T + T],
                    ).then_inc(bcs[i % 2].h, 16)
                for gt in range(32):
                    tn = 49 + gt
                    sy.wait_ge(ioev.h, ioev0 + tn + 1)
                    sy.dma_start(
                        out=gate_buf[gt * 128:(gt + 1) * 128, :],
                        in_=gr[:, (gt % 2) * T:(gt % 2) * T + T],
                    ).then_inc(gst[gt % 2].h, 16)
                sy.wait_ge(dts.h, dts0 + 5)
                sy.dma_start(out=acum_buf, in_=A_cum[:, :]).then_inc(sst.h, 16)
                for c in range(NCHUNK):
                    sy.wait_ge(dts.h, dts0 + 5 + (2 * c + 2) * 2)
                    sy.dma_start(out=alrow[0:1, c * 64:(c + 1) * 64],
                                 in_=nacT[127:128, (2 * c + 1) * 64:(2 * c + 2) * 64],
                                 ).then_inc(sst.h, 16)
                sy.wait_ge(dts.h, dts0 + 5 + 32 + 8 + 24)
                sy.dma_start(out=nacT_buf, in_=nacT[:, :]).then_inc(sst.h, 16)
                sy.dma_start(out=dt2T_buf, in_=dt2T[:, :]).then_inc(sst.h, 16)
                sy.dma_start(out=UD_buf, in_=UDs[:, :]).then_inc(sst.h, 16)
                for par in range(2):
                    sy.wait_ge(hst[par].h, hst0[par] + 16 * 16)
                    sy.wait_ge(bcs[par].h, bcs0[par] + 16 * 8)
                    sy.wait_ge(gst[par].h, gst0[par] + 16 * 16)
                sy.wait_ge(sst.h, sst0 + 16 * 8)

            @blk.gpsimd
            def _(g):
                g.nop()
                g.memset(ones64[:, :], 1.0).then_inc(lx.h, 1)
                for gi, (c0, cw, tiles) in enumerate(groups):
                    par = gi % 2
                    if gi >= 2:
                        g.wait_ge(wfr[par].h, wfr0[par] + gi // 2)
                    g.dma_start(
                        out=wsb[:, par * 16384:par * 16384 + 16 * cw].rearrange(
                            "p (k j) -> p k j", j=cw),
                        in_=wipT[:, c0:c0 + cw].rearrange("(k p) j -> p k j", p=128),
                    ).then_inc(wl[par].h, 16)

            @blk.tensor
            def _(t):
                t.wait_ge(lx.h, LX_ALL + 1)
                ntr = 0
                for gi, (c0, cw, tiles) in enumerate(groups):
                    par = gi % 2
                    t.wait_ge(wl[par].h, wl0[par] + 16 * (gi // 2 + 1))
                    for ti, tn in enumerate(tiles):
                        kind = tile_kind(tn)
                        ppar = tn % 2
                        if tn >= 2:
                            t.wait_ge(ioev.h, ioev0 + tn - 1)
                        cols = GROUPS_TH
                        last = None
                        mrows = 64 if kind == "dt" else 128
                        for (g0, gw) in cols:
                            for k in range(KCH):
                                last = t.matmul(
                                    ps_io[ppar][0:mrows, g0:g0 + gw],
                                    wsb[:, par * 16384 + k * cw + ti * 128:
                                        par * 16384 + k * cw + ti * 128 + mrows],
                                    xn_sb[:, k * TH + g0:k * TH + g0 + gw],
                                    start=(k == 0), stop=(k == KCH - 1))
                        last.then_inc(mmio.h, 1)
                        hc = tn - 3
                        if 0 <= hc < 32:
                            t.wait_ge(hhrd.h, hhrd0 + 2 * (hc + 1))
                            for tt in range(8):
                                n = hc * 8 + tt
                                if n > 0:
                                    t.wait_ge(trev.h, trev0 + n)
                                t.transpose(ps_tr[:, :],
                                            hh[:, (hc % 2) * T + tt * 128:
                                               (hc % 2) * T + tt * 128 + 128],
                                            ctx.C["ident_b16"][:, :]).then_inc(trd.h, 1)
                    t.nop().then_inc(wfr[par].h, 1)
                t.wait_ge(dts.h, dts0 + 5)
                for i in range(16):
                    src = nA_cum if i < 8 else dt2
                    tt = i % 8
                    if i > 0:
                        t.wait_ge(dts.h, dts0 + 5 + 2 * i)
                    t.transpose(ps_al[:, 0:64],
                                src[:, tt * 128:(tt + 1) * 128],
                                ctx.C["ident_f32"][0:64, 0:64]).then_inc(dts.h, 1)
                t.wait_ge(sst.h, sst0 + 16 * 5)
                t.wait_ge(dts.h, dts0 + 37)
                for c in range(NCHUNK):
                    if c > 0:
                        t.wait_ge(dts.h, dts0 + 5 + 32 + 2 * c)
                    t.matmul(ps_al[:, 0:64], ctx.C["onesrow_f32"][:, :],
                             alrow[0:1, c * 64:(c + 1) * 64],
                             start=True, stop=True).then_inc(dts.h, 1)

            @blk.scalar
            def _(s):
                s.wait_ge(lx.h, LX_ALL + 1)
                for tn in range(NT):
                    kind = tile_kind(tn)
                    ppar = tn % 2
                    s.wait_ge(mmio.h, mmio0 + tn + 1)
                    if kind == "dt":
                        s.activation(dt_sb[:, 0:512], ps_io[ppar][0:64, 3:515],
                                     AF.Copy)
                        s.activation(dt_sb[:, 512:1024], ps_io[ppar][0:64, 518:1030],
                                     AF.Copy).then_inc(ioev.h, 1)
                        s.activation(dtr[:, :], dt_sb[:, :], AF.Relu, bias=dtb[:, 0:1])
                        s.activation(dta[:, :], dt_sb[:, :], AF.Abs, bias=dtb[:, 0:1])
                        s.activation(dta[:, :], dta[:, :], AF.Exp, scale=-1.0)
                        s.activation(dta[:, :], dta[:, :], AF.Ln, bias=1.0)
                        s.activation(expA[:, :], alog[:, :], AF.Exp)
                        s.activation(nexpA[:, :], expA[:, :], AF.Copy,
                                     scale=-1.0).then_inc(dts.h, 1)
                    elif kind == "conv":
                        ch = tn - 1
                        if ch >= 2:
                            s.wait_ge(convd.h, convd0 + 2 * (ch - 1))
                        s.activation(cin[:, (ch % 2) * TH:(ch % 2) * TH + TH],
                                     ps_io[ppar][:, :], AF.Copy).then_inc(ioev.h, 1)
                        if ch >= 1:
                            chp = ch - 1
                            _emit_p3_silu(ctx, s, chp, convb, acc, hh, bc,
                                          convd, hhrd, convd0, hhrd0,
                                          hst, bcs, hst0, bcs0, trd, trd0)
                    else:
                        gt = tn - 49
                        if gt >= 2:
                            s.wait_ge(gst[gt % 2].h, gst0[gt % 2] + 16 * (gt // 2))
                        s.activation(gr[:, (gt % 2) * T:(gt % 2) * T + 512],
                                     ps_io[ppar][:, 3:515], AF.Copy)
                        s.activation(gr[:, (gt % 2) * T + 512:(gt % 2) * T + 1024],
                                     ps_io[ppar][:, 518:1030], AF.Copy).then_inc(ioev.h, 1)
                        if gt == 0:
                            _emit_p3_silu(ctx, s, 47, convb, acc, hh, bc,
                                          convd, hhrd, convd0, hhrd0,
                                          hst, bcs, hst0, bcs0, trd, trd0)
                for i in range(8):
                    s.wait_ge(dts.h, dts0 + 5 + 32 + 8 + 3 * i + 1)
                    s.activation(uexp[:, :], utmp[:, :], AF.Exp).then_inc(dts.h, 1)

            @blk.vector
            def _(v):
                for ch in range(48):
                    hc = ch - 3
                    if 0 <= hc < 32:
                        for tt in range(8):
                            n = hc * 8 + tt
                            v.wait_ge(trd.h, trd0 + n + 1)
                            if hc >= 2 and tt == 0:
                                v.wait_ge(hst[hc % 2].h, hst0[hc % 2] + 16 * (hc // 2))
                            v.tensor_copy(htr[:, (hc % 2) * 1024 + tt * 128:
                                              (hc % 2) * 1024 + tt * 128 + 128],
                                          ps_tr[:, :]).then_inc(trev.h, 1)
                    v.wait_ge(ioev.h, ioev0 + (ch + 1) + 1)
                    for seg in range(2):
                        ss0 = seg * 515
                        od = (ch % 2) * T + seg * 512
                        if ch >= 2:
                            v.wait_ge(hhrd.h, hhrd0 + 2 * (ch - 1))
                        v.tensor_scalar(
                            acc[:, od:od + 512],
                            cin[:, (ch % 2) * TH + ss0:(ch % 2) * TH + ss0 + 512],
                            convw[:, ch * 4:ch * 4 + 1], None, ALU.mult)
                        for kk in range(1, 4):
                            v.scalar_tensor_tensor(
                                acc[:, od:od + 512],
                                cin[:, (ch % 2) * TH + ss0 + kk:
                                    (ch % 2) * TH + ss0 + kk + 512],
                                convw[:, ch * 4 + kk:ch * 4 + kk + 1],
                                acc[:, od:od + 512],
                                ALU.mult, ALU.add).then_inc(
                                    convd.h, 1) if kk == 3 else \
                            v.scalar_tensor_tensor(
                                acc[:, od:od + 512],
                                cin[:, (ch % 2) * TH + ss0 + kk:
                                    (ch % 2) * TH + ss0 + kk + 512],
                                convw[:, ch * 4 + kk:ch * 4 + kk + 1],
                                acc[:, od:od + 512],
                                ALU.mult, ALU.add)
                v.wait_ge(dts.h, dts0 + 1)
                v.tensor_add(dt2[:, :], dtr[:, :], dta[:, :])
                v.tensor_scalar(A_c[:, :], dt2[:, :], nexpA[:, 0:1], None, ALU.mult)
                for c in range(NCHUNK):
                    v.tensor_tensor_scan(A_cum[:, c * 256:(c + 1) * 256],
                                         ones64[:, :], A_c[:, c * 256:(c + 1) * 256],
                                         0.0, ALU.mult, ALU.add)
                v.tensor_scalar(nA_cum[:, :], A_cum[:, :], -1.0, None,
                                ALU.mult).then_inc(dts.h, 4)
                for i in range(16):
                    dst = nacT if i < 8 else dt2T
                    tt = i % 8
                    v.wait_ge(dts.h, dts0 + 5 + 2 * i + 1)
                    v.tensor_copy(dst[:, tt * 64:(tt + 1) * 64],
                                  ps_al[:, 0:64]).then_inc(dts.h, 1)
                for c in range(NCHUNK):
                    v.wait_ge(dts.h, dts0 + 5 + 32 + 2 * c + 1)
                    v.tensor_copy(alrep[:, c * 64:(c + 1) * 64],
                                  ps_al[:, 0:64]).then_inc(dts.h, 1)
                for i in range(8):
                    c = i // 2
                    tt = i
                    base = dts0 + 5 + 32 + 8
                    v.wait_ge(dts.h, base + 3 * i)
                    v.tensor_sub(utmp[:, :], nacT[:, tt * 64:(tt + 1) * 64],
                                 alrep[:, c * 64:(c + 1) * 64]).then_inc(dts.h, 1)
                    v.wait_ge(dts.h, base + 3 * i + 2)
                    v.tensor_mul(UDs[:, tt * 64:(tt + 1) * 64], uexp[:, :],
                                 dt2T[:, tt * 64:(tt + 1) * 64]).then_inc(dts.h, 1)

    return


def _emit_p3_silu(ctx, s, ch, convb, acc, hh, bc, convd, hhrd, convd0, hhrd0,
                  hst, bcs, hst0, bcs0, trd, trd0):
    T_ = T
    for seg in range(2):
        s.wait_ge(convd.h, convd0 + 2 * ch + seg + 1)
        od = (ch % 2) * T_ + seg * 512
        if ch < 32:
            hc = ch
            if hc >= 2 and seg == 0:
                s.wait_ge(trd.h, trd0 + 8 * (hc - 1))
            s.activation(hh[:, od:od + 512], acc[:, od:od + 512],
                         AF.Silu, bias=convb[:, ch:ch + 1]).then_inc(hhrd.h, 1)
        else:
            i = ch - 32
            if i >= 2 and seg == 0:
                s.wait_ge(bcs[i % 2].h, bcs0[i % 2] + 16 * (i // 2))
            s.activation(bc[:, od:od + 512], acc[:, od:od + 512],
                         AF.Silu, bias=convb[:, ch:ch + 1]).then_inc(hhrd.h, 1)


def emit_p4(ctx, gate_buf, hidT_buf, B_buf, C_buf, acum_buf, nacT_buf,
            dt2T_buf, UD_buf, Drow_in, yg_buf, var_out):
    nc = ctx.nc
    with contextlib.ExitStack() as st:
        sb = lambda n, s, d: st.enter_context(nc.sbuf_tensor(n, s, d))
        psf = lambda n, s, d=F32: st.enter_context(nc.psum_tensor(n, s, d))
        hidT = sb("p4_hidT", [128, 8 * INTER], BF16)
        B_sb = sb("p4_B", [128, 8 * T], BF16)
        C_sb = sb("p4_C", [128, 8 * T], BF16)
        nacT = sb("p4_nacT", [128, 512], F32)
        dt2T = sb("p4_dt2T", [128, 512], F32)
        UDs = sb("p4_UD", [128, 512], F32)
        Drow = sb("p4_Drow", [1, 64], F32)
        Drep = sb("p4_Drep", [128, 64], F32)
        Iscl = sb("p4_Iscl", [128, 64 * 128], BF16)
        arow = sb("p4_arow", [4, 4096], F32)
        gring = sb("p4_gring", [128, 2 * 4 * 256], F32)
        yring = sb("p4_yring", [128, 2 * 4 * 256], BF16)
        Gsb = sb("p4_G", [128, 2 * 512], BF16)
        BTsb = sb("p4_BT", [128, 2 * 256], BF16)
        wrep = sb("p4_wrep", [128, 2 * 256], F32)
        tri0 = sb("p4_tri0", [128, 2 * 128], F32)
        tri1 = sb("p4_tri1", [128, 2 * 128], F32)
        W0b = sb("p4_W0", [128, 2 * 256], F32)
        W1b = sb("p4_W1", [128, 2 * 128], F32)
        M0b = sb("p4_M0", [128, 2 * 256], BF16)
        M1b = sb("p4_M1", [128, 2 * 128], BF16)
        Cw = sb("p4_Cw", [128, 2 * 256], BF16)
        hu = sb("p4_hu", [128, 2 * 128], BF16)
        stT = sb("p4_stT", [128, 2 * 64], BF16)
        sg = sb("p4_sg", [64, 2 * 256], F32)
        ygf = sb("p4_ygf", [64, 2 * 256], F32)
        sq = sb("p4_sq", [64, 2 * 256], BF16)
        vrow = sb("p4_vrow", [1, T], F32)
        ps_G = psf("p4_psG", [128, 512])
        ps_BT = psf("p4_psBT", [128, 256], BF16)
        ps_repl = psf("p4_psrepl", [128, 256])
        ps_st = psf("p4_psst", [128, 64])
        ps_Y = [psf("p4_psY0", [64, 256]), psf("p4_psY1", [64, 256])]
        ps_var = psf("p4_psvar", [1, T])

        lg = ctx.sem("p4_lg")
        acr = ctx.sem("p4_acr")
        gld = [ctx.sem("p4_gld0"), ctx.sem("p4_gld1")]
        gbt = ctx.sem("p4_gbt")
        gbte = ctx.sem("p4_gbte")
        mmr = ctx.sem("p4_mmr")
        dtri = ctx.sem("p4_dtri")
        wexp = ctx.sem("p4_wexp")
        wrp = ctx.sem("p4_wrp")
        cwd = ctx.sem("p4_cwd")
        mdone = ctx.sem("p4_mdone")
        huv = ctx.sem("p4_huv")
        stmm = ctx.sem("p4_stmm")
        stev = ctx.sem("p4_stev")
        ymm = ctx.sem("p4_ymm")
        sgd = ctx.sem("p4_sgd")
        yfd = ctx.sem("p4_yfd")
        sqd = ctx.sem("p4_sqd")
        ydb = ctx.sem("p4_ydb")
        vmm = ctx.sem("p4_vmm")
        yst = [ctx.sem("p4_yst0"), ctx.sem("p4_yst1")]
        vst = ctx.sem("p4_vst")
        z = dict(lg=lg.v, gbt=gbt.v, gbte=gbte.v, mmr=mmr.v,
                 dtri=dtri.v, wexp=wexp.v, wrp=wrp.v, cwd=cwd.v,
                 mdone=mdone.v, huv=huv.v, stmm=stmm.v, stev=stev.v,
                 ymm=ymm.v, sgd=sgd.v, yfd=yfd.v, sqd=sqd.v, ydb=ydb.v,
                 vmm=vmm.v, acr=acr.v)
        gld0 = [gld[0].v, gld[1].v]
        yst0 = [yst[0].v, yst[1].v]
        vst0 = vst.v
        LG_ALL = lg.v + 16 * 7
        NU = 256

        def uidx(c, g, hh):
            return c * 64 + g * 8 + hh

        with nc.Block() as blk:
            @blk.sync
            def _(sy):
                sy.dma_start(
                    out=hidT[:, :].rearrange("p (tt c) -> p tt c", c=INTER),
                    in_=hidT_buf.rearrange("(tt p) c -> p tt c", p=128),
                ).then_inc(lg.h, 16)
                sy.dma_start(
                    out=B_sb[:, :].rearrange("p (g t) -> p g t", t=T),
                    in_=B_buf.rearrange("(g p) t -> p g t", p=128),
                ).then_inc(lg.h, 16)
                sy.dma_start(
                    out=C_sb[:, :].rearrange("p (g t) -> p g t", t=T),
                    in_=C_buf.rearrange("(g p) t -> p g t", p=128),
                ).then_inc(lg.h, 16)
                sy.dma_start(out=nacT[:, :], in_=nacT_buf).then_inc(lg.h, 16)
                sy.dma_start(out=dt2T[:, :], in_=dt2T_buf).then_inc(lg.h, 16)
                sy.dma_start(out=UDs[:, :], in_=UD_buf).then_inc(lg.h, 16)
                sy.dma_start(out=Drow[:, :], in_=Drow_in).then_inc(lg.h, 16)
                def store_yg(cg):
                    c, g = cg // 8, cg % 8
                    sy.dma_start(
                        out=yg_buf[g * 512:(g + 1) * 512,
                                   c * 256:(c + 1) * 256].rearrange(
                            "(q p) t -> p q t", p=128),
                        in_=yring[:, (cg % 2) * 1024:(cg % 2) * 1024 + 1024
                                  ].rearrange("p (q t) -> p q t", t=256),
                    ).then_inc(yst[cg % 2].h, 16)

                for cg in range(32):
                    c, g = cg // 8, cg % 8
                    if g == 0:
                        if c > 0:
                            sy.wait_ge(mmr.h, z["mmr"] + 64 * c)
                        sy.dma_start(
                            out=arow[0:4, :].rearrange("q (hh i) -> q hh i", i=256),
                            in_=acum_buf[:, c * 256:(c + 1) * 256].rearrange(
                                "(q hh) i -> q hh i", hh=16),
                        ).then_inc(acr.h, 16)
                    if cg >= 2:
                        sy.wait_ge(sgd.h, z["sgd"] + 8 * (cg - 1))
                    sy.dma_start(
                        out=gring[:, (cg % 2) * 1024:(cg % 2) * 1024 + 1024
                                  ].rearrange("p (q t) -> p q t", t=256),
                        in_=gate_buf[g * 512:(g + 1) * 512,
                                     c * 256:(c + 1) * 256].rearrange(
                            "(q p) t -> p q t", p=128),
                    ).then_inc(gld[cg % 2].h, 16)
                    if cg >= 1:
                        sy.wait_ge(ydb.h, z["ydb"] + 8 * cg)
                        store_yg(cg - 1)
                sy.wait_ge(ydb.h, z["ydb"] + 8 * 32)
                store_yg(31)
                sy.wait_ge(vmm.h, z["vmm"] + NU + 1)
                sy.dma_start(out=var_out, in_=vrow[:, :]).then_inc(vst.h, 16)
                sy.wait_ge(vst.h, vst0 + 16)
                sy.wait_ge(yst[0].h, yst0[0] + 16 * 16)
                sy.wait_ge(yst[1].h, yst0[1] + 16 * 16)

            @blk.tensor
            def _(t):
                t.wait_ge(lg.h, LG_ALL)
                t.matmul(ps_st[:, 0:64], ctx.C["onesrow_f32"][:, :], Drow[0:1, :],
                         start=True, stop=True).then_inc(gbt.h, 1)
                for cg in range(32):
                    c, g = cg // 8, cg % 8
                    cpar = cg % 2
                    t.wait_ge(gbte.h, z["gbte"] + (1 + 4 * (cg - 1) if cg >= 2 else 1))
                    mmg = None
                    for jh in range(2):
                        mmg = t.matmul(
                            ps_G[:, jh * 256:jh * 256 + 256],
                            B_sb[:, g * T + c * 256 + jh * 128:
                                 g * T + c * 256 + jh * 128 + 128],
                            C_sb[:, g * T + c * 256:g * T + c * 256 + 256],
                            start=True, stop=True)
                    mmg.then_inc(gbt.h, 1)
                    for jh in range(2):
                        mmg = t.transpose(
                            ps_BT[:, jh * 128:jh * 128 + 128],
                            B_sb[:, g * T + c * 256 + jh * 128:
                                 g * T + c * 256 + jh * 128 + 128],
                            ctx.C["ident_b16"][:, :])
                    mmg.then_inc(gbt.h, 1)
                    for hh in range(8):
                        u = uidx(c, g, hh)
                        h = g * 8 + hh
                        upar = u % 2
                        if u > 0:
                            t.wait_ge(wexp.h, z["wexp"] + 3 * (u - 1) + 2)
                            t.wait_ge(dtri.h, z["dtri"] + 2 * u)
                        t.wait_ge(acr.h, z["acr"] + 16 * (c + 1))
                        t.matmul(ps_repl[:, :],
                                 ctx.C["sel4"][:, (h // 16) * 128:(h // 16) * 128 + 128],
                                 arow[0:4, (h % 16) * 256:(h % 16) * 256 + 256],
                                 start=True, stop=True).then_inc(mmr.h, 1)
                        t.wait_ge(gbte.h, z["gbte"] + 1 + 4 * cg + 4)
                        t.wait_ge(huv.h, z["huv"] + 2 * u + 2)
                        if u > 0:
                            t.wait_ge(stev.h, z["stev"] + u)
                        t.matmul(ps_st[:, :], BTsb[:, cpar * 256:cpar * 256 + 128],
                                 hu[:, upar * 128:upar * 128 + 64],
                                 start=True, stop=False)
                        t.matmul(ps_st[:, :], BTsb[:, cpar * 256 + 128:cpar * 256 + 256],
                                 hu[:, upar * 128 + 64:upar * 128 + 128],
                                 start=False, stop=True).then_inc(stmm.h, 1)
                        t.wait_ge(mdone.h, z["mdone"] + 2 * u + 2)
                        t.wait_ge(cwd.h, z["cwd"] + u + 1)
                        t.wait_ge(stev.h, z["stev"] + u + 1)
                        if u >= 2:
                            t.wait_ge(yfd.h, z["yfd"] + u - 1)
                        t.matmul(ps_Y[upar][:, :],
                                 hidT[:, (2 * c) * INTER + h * 64:
                                      (2 * c) * INTER + h * 64 + 64],
                                 M0b[:, upar * 256:upar * 256 + 256],
                                 start=True, stop=False)
                        t.matmul(ps_Y[upar][:, 128:256],
                                 hidT[:, (2 * c + 1) * INTER + h * 64:
                                      (2 * c + 1) * INTER + h * 64 + 64],
                                 M1b[:, upar * 128:upar * 128 + 128],
                                 start=False, stop=False)
                        t.matmul(ps_Y[upar][:, :],
                                 stT[:, upar * 64:upar * 64 + 64],
                                 Cw[:, upar * 256:upar * 256 + 256],
                                 start=False, stop=False)
                        t.matmul(ps_Y[upar][:, 0:128],
                                 hidT[:, (2 * c) * INTER + h * 64:
                                      (2 * c) * INTER + h * 64 + 64],
                                 Iscl[:, h * 128:(h + 1) * 128],
                                 start=False, stop=False)
                        t.matmul(ps_Y[upar][:, 128:256],
                                 hidT[:, (2 * c + 1) * INTER + h * 64:
                                      (2 * c + 1) * INTER + h * 64 + 64],
                                 Iscl[:, h * 128:(h + 1) * 128],
                                 start=False, stop=True).then_inc(ymm.h, 1)
                        if u >= 2:
                            up = u - 2
                            cp = up // 64
                            t.wait_ge(sqd.h, z["sqd"] + up + 1)
                            t.matmul(ps_var[0:1, cp * 256:cp * 256 + 256],
                                     ctx.C["ones_b16"][0:64, 0:1],
                                     sq[:, (up % 2) * 256:(up % 2) * 256 + 256],
                                     start=(up % 64 == 0), stop=(up % 64 == 63)).then_inc(vmm.h, 1)
                for up in (NU - 2, NU - 1):
                    cp = up // 64
                    t.wait_ge(sqd.h, z["sqd"] + up + 1)
                    t.matmul(ps_var[0:1, cp * 256:cp * 256 + 256],
                             ctx.C["ones_b16"][0:64, 0:1],
                             sq[:, (up % 2) * 256:(up % 2) * 256 + 256],
                             start=(up % 64 == 0), stop=(up % 64 == 63)).then_inc(vmm.h, 1)

            @blk.vector
            def _(v):
                v.wait_ge(lg.h, LG_ALL)
                v.wait_ge(gbt.h, z["gbt"] + 1)
                v.tensor_copy(Drep[:, :], ps_st[:, 0:64])
                for h in range(H):
                    ti = v.tensor_scalar(Iscl[:, h * 128:(h + 1) * 128],
                                         ctx.C["ident_b16"][:, :],
                                         Drep[:, h:h + 1], None, ALU.mult)
                ti.then_inc(gbte.h, 1)
                for cg in range(32):
                    c, g = cg // 8, cg % 8
                    cpar = cg % 2
                    v.wait_ge(gbt.h, z["gbt"] + 1 + 2 * cg + 1)
                    if cg >= 2:
                        v.wait_ge(mdone.h, z["mdone"] + 2 * (8 * (cg - 1)))
                    v.tensor_copy(Gsb[:, cpar * 512:cpar * 512 + 512],
                                  ps_G[:, :]).then_inc(gbte.h, 1)
                    v.wait_ge(gbt.h, z["gbt"] + 1 + 2 * cg + 2)
                    if cg >= 2:
                        v.wait_ge(stmm.h, z["stmm"] + 8 * (cg - 1))
                    v.tensor_copy(BTsb[:, cpar * 256:cpar * 256 + 256],
                                  ps_BT[:, :]).then_inc(gbte.h, 3)
                    for hh in range(8):
                        u = uidx(c, g, hh)
                        h = g * 8 + hh
                        upar = u % 2
                        v.wait_ge(mmr.h, z["mmr"] + u + 1)
                        if u >= 2:
                            v.wait_ge(wexp.h, z["wexp"] + 3 * (u - 1))
                        v.tensor_add(tri0[:, upar * 128:upar * 128 + 128],
                                     ps_repl[:, 0:128],
                                     ctx.C["tri_f32"][:, :]).then_inc(dtri.h, 1)
                        v.tensor_add(tri1[:, upar * 128:upar * 128 + 128],
                                     ps_repl[:, 128:256],
                                     ctx.C["tri_f32"][:, :]).then_inc(dtri.h, 1)
                        v.wait_ge(wrp.h, z["wrp"] + u + 1)
                        if u >= 2:
                            v.wait_ge(ymm.h, z["ymm"] + u - 1)
                        v.tensor_mul(Cw[:, upar * 256:upar * 256 + 256],
                                     C_sb[:, g * T + c * 256:g * T + c * 256 + 256],
                                     wrep[:, upar * 256:upar * 256 + 256]
                                     ).then_inc(cwd.h, 1)
                        v.tensor_scalar(hu[:, upar * 128:upar * 128 + 64],
                                        hidT[:, (2 * c) * INTER + h * 64:
                                             (2 * c) * INTER + h * 64 + 64],
                                        UDs[:, (2 * c) * 64 + h:(2 * c) * 64 + h + 1],
                                        None, ALU.mult).then_inc(huv.h, 1)
                        v.tensor_scalar(hu[:, upar * 128 + 64:upar * 128 + 128],
                                        hidT[:, (2 * c + 1) * INTER + h * 64:
                                             (2 * c + 1) * INTER + h * 64 + 64],
                                        UDs[:, (2 * c + 1) * 64 + h:
                                            (2 * c + 1) * 64 + h + 1],
                                        None, ALU.mult).then_inc(huv.h, 1)
                        v.wait_ge(wexp.h, z["wexp"] + 3 * u + 2)
                        v.scalar_tensor_tensor(
                            M0b[:, upar * 256:upar * 256 + 256],
                            W0b[:, upar * 256:upar * 256 + 256],
                            dt2T[:, (2 * c) * 64 + h:(2 * c) * 64 + h + 1],
                            Gsb[:, cpar * 512:cpar * 512 + 256],
                            ALU.mult, ALU.mult).then_inc(mdone.h, 1)
                        v.wait_ge(wexp.h, z["wexp"] + 3 * u + 3)
                        v.scalar_tensor_tensor(
                            M1b[:, upar * 128:upar * 128 + 128],
                            W1b[:, upar * 128:upar * 128 + 128],
                            dt2T[:, (2 * c + 1) * 64 + h:(2 * c + 1) * 64 + h + 1],
                            Gsb[:, cpar * 512 + 384:cpar * 512 + 512],
                            ALU.mult, ALU.mult).then_inc(mdone.h, 1)
                        v.wait_ge(ymm.h, z["ymm"] + u + 1)
                        v.wait_ge(sgd.h, z["sgd"] + u + 1)
                        v.tensor_mul(ygf[:, upar * 256:upar * 256 + 256],
                                     ps_Y[upar][:, :],
                                     sg[:, upar * 256:upar * 256 + 256]
                                     ).then_inc(yfd.h, 1)
                        if cg >= 2 and hh == 0:
                            v.wait_ge(yst[cg % 2].h, yst0[cg % 2] + 16 * (cg // 2))
                        v.tensor_copy(
                            yring[64 * (hh % 2):64 * (hh % 2) + 64,
                                  (cg % 2) * 1024 + (hh // 2) * 256:
                                  (cg % 2) * 1024 + (hh // 2) * 256 + 256],
                            ygf[:, upar * 256:upar * 256 + 256]).then_inc(ydb.h, 1)
                v.wait_ge(vmm.h, z["vmm"] + NU)
                v.tensor_copy(vrow[:, :], ps_var[0:1, :]).then_inc(vmm.h, 1)

            @blk.scalar
            def _(s):
                s.wait_ge(lg.h, LG_ALL)
                for cg in range(32):
                    c, g = cg // 8, cg % 8
                    for hh in range(8):
                        u = uidx(c, g, hh)
                        h = g * 8 + hh
                        upar = u % 2
                        s.wait_ge(mmr.h, z["mmr"] + u + 1)
                        if u >= 2:
                            s.wait_ge(cwd.h, z["cwd"] + u - 1)
                        s.activation(wrep[:, upar * 256:upar * 256 + 256],
                                     ps_repl[:, :], AF.Exp).then_inc(wrp.h, 1)
                        s.wait_ge(dtri.h, z["dtri"] + 2 * u + 1)
                        if u >= 2:
                            s.wait_ge(mdone.h, z["mdone"] + 2 * (u - 1))
                        s.activation(W0b[:, upar * 256:upar * 256 + 128],
                                     tri0[:, upar * 128:upar * 128 + 128], AF.Exp,
                                     bias=nacT[:, (2 * c) * 64 + h:
                                               (2 * c) * 64 + h + 1]).then_inc(wexp.h, 1)
                        s.activation(W0b[:, upar * 256 + 128:upar * 256 + 256],
                                     ps_repl[:, 128:256], AF.Exp,
                                     bias=nacT[:, (2 * c) * 64 + h:
                                               (2 * c) * 64 + h + 1]).then_inc(wexp.h, 1)
                        s.wait_ge(dtri.h, z["dtri"] + 2 * u + 2)
                        s.activation(W1b[:, upar * 128:upar * 128 + 128],
                                     tri1[:, upar * 128:upar * 128 + 128], AF.Exp,
                                     bias=nacT[:, (2 * c + 1) * 64 + h:
                                               (2 * c + 1) * 64 + h + 1]
                                     ).then_inc(wexp.h, 1)
                        s.wait_ge(stmm.h, z["stmm"] + u + 1)
                        if u >= 2:
                            s.wait_ge(ymm.h, z["ymm"] + u - 1)
                        s.activation(stT[:, upar * 64:upar * 64 + 64],
                                     ps_st[:, :], AF.Copy).then_inc(stev.h, 1)
                        s.wait_ge(gld[cg % 2].h, gld0[cg % 2] + 16 * (cg // 2 + 1))
                        if u >= 2:
                            s.wait_ge(yfd.h, z["yfd"] + u - 1)
                        s.activation(
                            sg[:, upar * 256:upar * 256 + 256],
                            gring[64 * (hh % 2):64 * (hh % 2) + 64,
                                  (cg % 2) * 1024 + (hh // 2) * 256:
                                  (cg % 2) * 1024 + (hh // 2) * 256 + 256],
                            AF.Silu).then_inc(sgd.h, 1)
                        s.wait_ge(yfd.h, z["yfd"] + u + 1)
                        if u >= 2:
                            s.wait_ge(vmm.h, z["vmm"] + u - 1)
                        s.activation(sq[:, upar * 256:upar * 256 + 256],
                                     ygf[:, upar * 256:upar * 256 + 256],
                                     AF.Square).then_inc(sqd.h, 1)
    return


def emit_p5(ctx, yg_buf, var_in, wopT, normw_in, xffn_buf, wgateT_in,
            x1_buf, xn2_buf, ind_buf, top1_buf):
    nc = ctx.nc
    with contextlib.ExitStack() as st:
        sb = lambda n, s, d: st.enter_context(nc.sbuf_tensor(n, s, d))
        psf = lambda n, s, d=F32: st.enter_context(nc.psum_tensor(n, s, d))
        yg = sb("p5_yg", [128, 32 * T], BF16)
        x1 = sb("p5_x1", [128, 16 * T], F32)
        wsb = sb("p5_w", [128, 2 * 32 * 128], BF16)
        normw = sb("p5_nw", [128, 32], F32)
        vrow = sb("p5_vrow", [1, T], F32)
        rrow = sb("p5_rrow", [1, T], F32)
        rrep = sb("p5_rrep", [128, T], F32)
        t1 = sb("p5_t1", [128, 512], F32)
        x1b = sb("p5_x1b", [128, 2 * T], BF16)
        sqb = sb("p5_sqb", [128, 2 * T], BF16)
        mrep = sb("p5_mrep", [128, T], F32)
        vtmp = sb("p5_vtmp", [128, T], F32)
        r2rep = sb("p5_r2rep", [128, T], F32)
        xn2f = sb("p5_xn2f", [128, 2 * T], F32)
        xn2b = sb("p5_xn2b", [128, 2 * T], BF16)
        wg_sb = sb("p5_wg", [128, 16 * 16], F32)
        gl_sb = sb("p5_gl", [16, T], F32)
        glT = sb("p5_glT", [128, 8 * 16], F32)
        rmaxn = sb("p5_rmaxn", [128, 8], F32)
        rmaxp = sb("p5_rmaxp", [128, 8], F32)
        eexp = sb("p5_eexp", [128, 128], F32)
        ssum = sb("p5_ssum", [128, 8], F32)
        top1 = sb("p5_top1", [128, 8], F32)
        rtmp = sb("p5_rtmp", [128, 8], F32)
        indt = sb("p5_ind", [128, 8 * 16], BF16)
        ps_op = [psf("p5_psop0", [128, T]), psf("p5_psop1", [128, T])]
        ps_s = psf("p5_pss", [128, T])
        ps_q = psf("p5_psq", [128, T])

        ld = ctx.sem("p5_ld")
        wl = [ctx.sem("p5_wl0"), ctx.sem("p5_wl1")]
        wsc = [ctx.sem("p5_wsc0"), ctx.sem("p5_wsc1")]
        wfr = [ctx.sem("p5_wf0"), ctx.sem("p5_wf1")]
        xfl = ctx.sem("p5_xfl")
        ms = ctx.sem("p5_ms")
        mmo = ctx.sem("p5_mmo")
        oev = ctx.sem("p5_oev")
        x1bd = ctx.sem("p5_x1bd")
        sqd = ctx.sem("p5_sqd")
        smm = ctx.sem("p5_smm")
        x1s = ctx.sem("p5_x1s")
        nrm = ctx.sem("p5_nrm")
        gmm = ctx.sem("p5_gmm")
        xst = [ctx.sem("p5_xst0"), ctx.sem("p5_xst1")]
        z = dict(ld=ld.v, xfl=xfl.v, ms=ms.v, mmo=mmo.v, oev=oev.v,
                 x1bd=x1bd.v, sqd=sqd.v, smm=smm.v, x1s=x1s.v, nrm=nrm.v,
                 gmm=gmm.v)
        wl0 = [wl[0].v, wl[1].v]
        wsc0 = [wsc[0].v, wsc[1].v]
        wfr0 = [wfr[0].v, wfr[1].v]
        xst0 = [xst[0].v, xst[1].v]
        LD_ALL = ld.v + 16 * 11
        M_RROW, M_RMM, M_REV, M_SPS, M_MEAN, M_VAR, M_R2 = (
            z["ms"] + 1, z["ms"] + 2, z["ms"] + 3, z["ms"] + 4, z["ms"] + 5,
            z["ms"] + 6, z["ms"] + 7)
        G_PSGL = z["gmm"] + 1
        G_GLEV = z["gmm"] + 2
        def G_TR(tt): return z["gmm"] + 2 + 2 * tt + 1
        def G_TREV(tt): return z["gmm"] + 2 + 2 * tt + 2
        G_T0 = z["gmm"] + 18
        G_DVE1 = G_T0 + 8 + 16
        G_LN = G_T0 + 8 + 17
        G_FIN = G_T0 + 8 + 18

        with nc.Block() as blk:
            @blk.sync
            def _(sy):
                for q in range(8):
                    sy.dma_start(
                        out=yg[:, q * 4 * T:(q + 1) * 4 * T].rearrange(
                            "p (k t) -> p k t", t=T),
                        in_=yg_buf[q * 512:(q + 1) * 512, :].rearrange(
                            "(k p) t -> p k t", p=128),
                    ).then_inc(ld.h, 16)
                sy.dma_start(out=vrow[:, :], in_=var_in).then_inc(ld.h, 16)
                sy.dma_start(out=normw[:, :], in_=normw_in).then_inc(ld.h, 16)
                sy.dma_start(out=wg_sb[:, :].rearrange("p (k j) -> p k j", j=16),
                             in_=wgateT_in.rearrange("(k p) j -> p k j", p=128),
                             ).then_inc(ld.h, 16)
                for ot in range(16):
                    sy.dma_start(out=x1[:, ot * T:(ot + 1) * T],
                                 in_=xffn_buf[ot * 128:(ot + 1) * 128, :],
                                 ).then_inc(xfl.h, 16)
                for ot in range(16):
                    sy.wait_ge(oev.h, z["oev"] + ot + 1)
                    sy.dma_start(out=x1_buf[ot * 128:(ot + 1) * 128, :],
                                 in_=x1[:, ot * T:(ot + 1) * T]).then_inc(x1s.h, 16)
                for k in range(KCH):
                    sy.wait_ge(nrm.h, z["nrm"] + 2 * (k + 1))
                    sy.dma_start(out=xn2_buf[k * 128:(k + 1) * 128, :],
                                 in_=xn2b[:, (k % 2) * T:(k % 2) * T + T],
                                 ).then_inc(xst[k % 2].h, 16)
                sy.wait_ge(gmm.h, G_FIN)
                sy.dma_start(out=ind_buf.rearrange("(tt p) h -> p tt h", p=128),
                             in_=indt[:, :].rearrange("p (tt h) -> p tt h", h=16),
                             ).then_inc(xst[0].h, 16)
                sy.dma_start(out=top1_buf, in_=top1[:, :]).then_inc(xst[0].h, 16)
                sy.wait_ge(xst[0].h, xst0[0] + 16 * 10)
                sy.wait_ge(xst[1].h, xst0[1] + 16 * 8)
                sy.wait_ge(x1s.h, z["x1s"] + 16 * 16)

            @blk.gpsimd
            def _(g):
                for og in range(16):
                    par = og % 2
                    if og >= 2:
                        g.wait_ge(wfr[par].h, wfr0[par] + og // 2)
                    g.dma_start(
                        out=wsb[:, par * 4096:(par + 1) * 4096].rearrange(
                            "p (k j) -> p k j", j=128),
                        in_=wopT[:, og * 128:(og + 1) * 128].rearrange(
                            "(k p) j -> p k j", p=128),
                    ).then_inc(wl[par].h, 16)

            @blk.tensor
            def _(t):
                t.wait_ge(ld.h, LD_ALL)
                t.wait_ge(ms.h, M_RROW)
                mmx = None
                for hf in range(2):
                    mmx = t.matmul(ps_s[:, hf * 512:hf * 512 + 512],
                                   ctx.C["onesrow_f32"][:, :],
                                   rrow[0:1, hf * 512:hf * 512 + 512],
                                   start=True, stop=True)
                mmx.then_inc(ms.h, 1)
                t.wait_ge(ms.h, M_REV)
                for ot in range(16):
                    par = ot % 2
                    ppar = ot % 2
                    t.wait_ge(wl[par].h, wl0[par] + 16 * (ot // 2 + 1))
                    t.wait_ge(wsc[par].h, wsc0[par] + ot // 2 + 1)
                    if ot >= 2:
                        t.wait_ge(oev.h, z["oev"] + ot - 1)
                    last = None
                    for hf in range(2):
                        for k in range(32):
                            last = t.matmul(
                                ps_op[ppar][:, hf * 512:hf * 512 + 512],
                                wsb[:, par * 4096 + k * 128:
                                    par * 4096 + k * 128 + 128],
                                yg[:, k * T + hf * 512:k * T + hf * 512 + 512],
                                start=(k == 0), stop=(k == 31))
                    last.then_inc(mmo.h, 1)
                    t.nop().then_inc(wfr[par].h, 1)
                    if ot >= 1:
                        otp = ot - 1
                        t.wait_ge(x1bd.h, z["x1bd"] + otp + 1)
                        t.wait_ge(sqd.h, z["sqd"] + otp + 1)
                        mms_ = None
                        for hf in range(2):
                            t.matmul(ps_s[:, hf * 512:hf * 512 + 512],
                                     ctx.C["ones_b16"][:, :],
                                     x1b[:, (otp % 2) * T + hf * 512:
                                         (otp % 2) * T + hf * 512 + 512],
                                     start=(otp == 0), stop=(otp == 15))
                            mms_ = t.matmul(ps_q[:, hf * 512:hf * 512 + 512],
                                     ctx.C["ones_b16"][:, :],
                                     sqb[:, (otp % 2) * T + hf * 512:
                                         (otp % 2) * T + hf * 512 + 512],
                                     start=(otp == 0), stop=(otp == 15))
                        mms_.then_inc(smm.h, 1)
                for otp in (15,):
                    t.wait_ge(x1bd.h, z["x1bd"] + otp + 1)
                    t.wait_ge(sqd.h, z["sqd"] + otp + 1)
                    mmb = None
                    for hf in range(2):
                        t.matmul(ps_s[:, hf * 512:hf * 512 + 512],
                                 ctx.C["ones_b16"][:, :],
                                 x1b[:, (otp % 2) * T + hf * 512:
                                     (otp % 2) * T + hf * 512 + 512],
                                 start=False, stop=True)
                        mmb = t.matmul(ps_q[:, hf * 512:hf * 512 + 512],
                                       ctx.C["ones_b16"][:, :],
                                       sqb[:, (otp % 2) * T + hf * 512:
                                           (otp % 2) * T + hf * 512 + 512],
                                       start=False, stop=True)
                    mmb.then_inc(ms.h, 1)
                    t.nop().then_inc(smm.h, 1)
                for k in range(KCH):
                    t.wait_ge(nrm.h, z["nrm"] + 2 * k + 1)
                    mmg = None
                    for hf in range(2):
                        mmg = t.matmul(ps_op[1][0:16, hf * 512:hf * 512 + 512],
                                       wg_sb[:, k * 16:(k + 1) * 16],
                                       xn2f[:, (k % 2) * T + hf * 512:
                                            (k % 2) * T + hf * 512 + 512],
                                       start=(k == 0), stop=(k == KCH - 1))
                    if k == KCH - 1:
                        mmg.then_inc(gmm.h, 1)
                    else:
                        mmg.then_inc(mmo.h, 1)
                t.wait_ge(gmm.h, G_GLEV)
                for tt in range(8):
                    if tt > 0:
                        t.wait_ge(gmm.h, G_TREV(tt - 1))
                    t.transpose(ps_op[0][:, 0:16], gl_sb[:, tt * 128:(tt + 1) * 128],
                                ctx.C["ident_f32"][0:16, 0:16]).then_inc(gmm.h, 1)

            @blk.scalar
            def _(s):
                s.wait_ge(ld.h, z["ld"] + 16 * 9)
                s.activation(rrow[:, :], vrow[:, :], AF.Ln,
                             bias=ctx.C["eps_rms"][0:1, 0:1], scale=1.0 / INTER)
                s.activation(rrow[:, :], rrow[:, :], AF.Exp,
                             scale=-0.5).then_inc(ms.h, 1)
                s.wait_ge(ms.h, M_RMM)
                s.activation(rrep[:, :], ps_s[:, :], AF.Copy).then_inc(ms.h, 1)
                for ot in range(16):
                    s.wait_ge(x1bd.h, z["x1bd"] + ot + 1)
                    if ot >= 2:
                        s.wait_ge(smm.h, z["smm"] + ot - 1)
                    s.activation(sqb[:, (ot % 2) * T:(ot % 2) * T + T],
                                 x1b[:, (ot % 2) * T:(ot % 2) * T + T],
                                 AF.Square).then_inc(sqd.h, 1)
                s.wait_ge(ms.h, M_SPS)
                s.activation(mrep[:, :], ps_s[:, :], AF.Copy, scale=1.0 / D_MODEL)
                s.activation(vtmp[:, :], mrep[:, :], AF.Square).then_inc(ms.h, 1)
                s.wait_ge(ms.h, M_VAR)
                s.activation(r2rep[:, :], vtmp[:, :], AF.Ln,
                             bias=ctx.C["eps_ln"][:, 0:1])
                s.activation(r2rep[:, :], r2rep[:, :], AF.Exp,
                             scale=-0.5).then_inc(ms.h, 1)
                s.wait_ge(gmm.h, G_PSGL)
                s.activation(gl_sb[:, :], ps_op[1][0:16, :], AF.Copy).then_inc(gmm.h, 1)
                s.wait_ge(gmm.h, G_TREV(7))
                for tt in range(8):
                    s.activation(eexp[:, tt * 16:(tt + 1) * 16],
                                 glT[:, tt * 16:(tt + 1) * 16],
                                 AF.Exp).then_inc(gmm.h, 1)
                s.wait_ge(gmm.h, G_DVE1)
                s.activation(eexp[:, 0:16], glT[:, 0:16], AF.Exp)
                s.activation(eexp[:, 0:16], glT[:, 0:16], AF.Exp)
                s.activation(rmaxn[:, :], ssum[:, 0:8], AF.Ln)
                s.activation(eexp[:, 0:16], glT[:, 0:16], AF.Exp)
                s.activation(rtmp[:, :], rmaxn[:, 0:8], AF.Exp,
                             scale=-1.0).then_inc(gmm.h, 1)

            @blk.vector
            def _(v):
                v.memset(ssum[:, :], 0.0)
                v.wait_ge(ld.h, z["ld"] + 16 * 10)
                v.wait_ge(ms.h, M_REV)
                v.wait_ge(xfl.h, z["xfl"] + 16 * 16)

                def scale_slot(og):
                    par = og % 2
                    v.wait_ge(wl[par].h, wl0[par] + 16 * (og // 2 + 1))
                    ti = None
                    for k in range(32):
                        ti = v.tensor_scalar(
                            wsb[:, par * 4096 + k * 128:par * 4096 + k * 128 + 128],
                            wsb[:, par * 4096 + k * 128:par * 4096 + k * 128 + 128],
                            normw[:, k:k + 1], None, ALU.mult)
                    ti.then_inc(wsc[par].h, 1)

                def evac(ot):
                    ppar = ot % 2
                    v.wait_ge(mmo.h, z["mmo"] + ot + 1)
                    ti = None
                    for hf in range(2):
                        v.tensor_mul(t1[:, :],
                                     ps_op[ppar][:, hf * 512:hf * 512 + 512],
                                     rrep[:, hf * 512:hf * 512 + 512])
                        ti = v.tensor_add(
                            x1[:, ot * T + hf * 512:ot * T + hf * 512 + 512],
                            x1[:, ot * T + hf * 512:ot * T + hf * 512 + 512],
                            t1[:, :])
                    ti.then_inc(oev.h, 1)
                    if ot >= 2:
                        v.wait_ge(sqd.h, z["sqd"] + ot - 1)
                    v.tensor_copy(x1b[:, (ot % 2) * T:(ot % 2) * T + T],
                                  x1[:, ot * T:(ot + 1) * T]).then_inc(x1bd.h, 1)

                for ot in range(16):
                    scale_slot(ot)
                    if ot >= 1:
                        evac(ot - 1)
                evac(15)
                v.wait_ge(ms.h, M_MEAN)
                v.scalar_tensor_tensor(vtmp[:, :], ps_q[:, :], 1.0 / D_MODEL,
                                       vtmp[:, :], ALU.mult,
                                       ALU.subtract).then_inc(ms.h, 1)
                v.wait_ge(ms.h, M_R2)
                for k in range(KCH):
                    if k >= 2:
                        v.wait_ge(mmo.h, z["mmo"] + 16 + k - 1)
                        v.wait_ge(xst[k % 2].h, xst0[k % 2] + 16 * (k // 2))
                    v.tensor_sub(xn2f[:, (k % 2) * T:(k % 2) * T + T],
                                 x1[:, k * T:(k + 1) * T], mrep[:, :])
                    v.tensor_mul(xn2f[:, (k % 2) * T:(k % 2) * T + T],
                                 xn2f[:, (k % 2) * T:(k % 2) * T + T],
                                 r2rep[:, :]).then_inc(nrm.h, 1)
                    v.tensor_copy(xn2b[:, (k % 2) * T:(k % 2) * T + T],
                                  xn2f[:, (k % 2) * T:(k % 2) * T + T],
                                  ).then_inc(nrm.h, 1)
                for tt in range(8):
                    v.wait_ge(gmm.h, G_TR(tt))
                    v.tensor_copy(glT[:, tt * 16:(tt + 1) * 16],
                                  ps_op[0][:, 0:16]).then_inc(gmm.h, 1)
                v.wait_ge(gmm.h, G_T0 + 8)
                for tt in range(8):
                    v.reduce_max(rmaxp[:, tt:tt + 1], glT[:, tt * 16:(tt + 1) * 16],
                                 axis=AX)
                    v.reduce_sum(ssum[:, tt:tt + 1],
                                 eexp[:, tt * 16:(tt + 1) * 16], axis=AX)
                    v.reduce_max(top1[:, tt:tt + 1],
                                 eexp[:, tt * 16:(tt + 1) * 16], axis=AX)
                    v.tensor_scalar(indt[:, tt * 16:(tt + 1) * 16],
                                    glT[:, tt * 16:(tt + 1) * 16],
                                    rmaxp[:, tt:tt + 1], None,
                                    ALU.is_equal).then_inc(gmm.h, 2)
                v.wait_ge(gmm.h, G_LN)
                import os as _o
                if _o.environ.get("P5DBG") == "rtmp":
                    v.tensor_copy(top1[:, 0:8], rtmp[:, 0:8]).then_inc(gmm.h, 1)
                elif _o.environ.get("P5DBG") == "ssum":
                    v.tensor_copy(top1[:, 0:8], ssum[:, 0:8]).then_inc(gmm.h, 1)
                elif _o.environ.get("P5DBG") == "emax":
                    v.nop().then_inc(gmm.h, 1)
                else:
                    v.tensor_mul(top1[:, 0:8], top1[:, 0:8],
                                 rtmp[:, 0:8]).then_inc(gmm.h, 1)
    return



def emit_L1(ctx, xn2_buf, x1_buf, wqT, wkT, wvT, woT, bq_col, bo_col, brow_in,
            wqkvT, ind_buf, top1_buf, flags_in, ar_in, ar_out, ag_in, ag_out,
            x2a_buf, out_buf, groups2):
    nc = ctx.nc

    def B(tag="b"):
        barrier(ctx, "L1bar")

    psq = [ctx.ps(f"L1_ps{i}", [128, 512], F32) for i in range(4)]
    ps_tb = ctx.ps("L1_pstb", [128, 512], BF16)
    ld = ctx.sem("L1_ld")
    wl = ctx.sem("L1_wl")
    cc = ctx.sem("L1_cc")
    pe = ctx.sem("L1_pe")
    c30 = ctx.sb("L1_c30", [128, 1], F32)

    def dma(pairs, tag="d"):
        with nc.Block() as blk:
            @blk.sync
            def _(sy):
                for (o, i) in pairs:
                    sy.dma_start(out=o, in_=i).then_inc(ld.h, 16)
                    ld.v += 16
                sy.wait_ge(ld.h, ld.v)
        B(tag)

    def wload(wap, c0, wsb, width=512):
        with nc.Block() as blk:
            @blk.gpsimd
            def _(g):
                g.dma_start(
                    out=wsb[:, 0:16 * width].rearrange("p (k j) -> p k j", j=width),
                    in_=wap[:, c0:c0 + width].rearrange("(k p) j -> p k j", p=128),
                ).then_inc(wl.h, 16)
                wl.v += 16
                g.wait_ge(wl.h, wl.v)
        B("w")

    with nc.Block() as blk:
        @blk.gpsimd
        def _(g):
            g.memset(c30[:, :], -30.0)
    B("c30")

    with contextlib.ExitStack() as sA:
        sbA = lambda n, s, d: sA.enter_context(nc.sbuf_tensor(n, s, d))
        xn2 = sbA("L1a_xn2", [128, KCH * T], BF16)
        Q_fm = sbA("L1a_Q", [128, KCH * T], BF16)
        at1 = sbA("L1a_at1", [128, KCH * T], BF16)
        kvp = sbA("L1a_kvp", [128, 2064], F32)
        bq = sbA("L1a_bq", [128, 16], F32)
        bqn = sbA("L1a_bqn", [128, 16], F32)
        bo = sbA("L1a_bo", [128, 16], F32)
        brow = sbA("L1a_brow", [1, 512], F32)

        dma([(xn2[:, k * T:(k + 1) * T], xn2_buf[k * 128:(k + 1) * 128, :])
             for k in range(KCH)], "xn2")
        dma([(bq[:, :], bq_col), (bo[:, :], bo_col)], "b1")
        with nc.Block() as blk:
            @blk.vector
            def _(v):
                v.tensor_scalar(bqn[:, :], bq[:, :], -1.0, None, ALU.mult)
        B("bqn")

        with contextlib.ExitStack() as sA1:
            sb1 = lambda n, s, d: sA1.enter_context(nc.sbuf_tensor(n, s, d))
            wsb = sb1("L1a_w", [128, 16 * 512], BF16)
            K_tm = sb1("L1a_K", [128, 8 * D_MODEL], BF16)
            V_tm = sb1("L1a_V", [128, 8 * D_MODEL], BF16)
            tq = sb1("L1a_tq", [128, 4 * 512], F32)
            tq2 = sb1("L1a_tq2", [128, 4 * 512], F32)

            for m in range(3):
                wap = (wqT, wkT, wvT)[m]
                for og in range(4):
                    wload(wap, og * 512, wsb)
                    if m >= 1:
                        o0 = (m - 1) * D_MODEL + og * 512
                        dma([(brow[0:1, :], brow_in[0:1, o0:o0 + 512])], "br")
                    for half in range(2):
                        with nc.Block() as blk:
                            @blk.tensor
                            def _(t):
                                mm = None
                                if m == 0:
                                    for oti in range(2):
                                        for hf in range(2):
                                            for k in range(KCH):
                                                mm = t.matmul(
                                                    psq[oti * 2 + hf][:, :],
                                                    wsb[:, k * 512 + (half * 2 + oti) * 128:
                                                        k * 512 + (half * 2 + oti) * 128 + 128],
                                                    xn2[:, k * T + hf * 512:
                                                        k * T + hf * 512 + 512],
                                                    start=(k == 0), stop=(k == KCH - 1))
                                else:
                                    for j in range(4):
                                        tt = half * 4 + j
                                        t.matmul(psq[j][:, :],
                                                 ctx.C["onesrow_f32"][:, :],
                                                 brow[0:1, 0:512],
                                                 start=True, stop=False)
                                        for k in range(KCH):
                                            mm = t.matmul(
                                                psq[j][:, :],
                                                xn2[:, k * T + tt * 128:
                                                    k * T + tt * 128 + 128],
                                                wsb[:, k * 512:(k + 1) * 512],
                                                start=False, stop=(k == KCH - 1))
                                pe.mark(mm, 1)
                        B("mm")
                        with nc.Block() as blk:
                            @blk.scalar
                            def _(s):
                                s.wait_ge(pe.h, pe.v)
                                for j in range(4):
                                    if m == 0:
                                        ot = og * 4 + half * 2 + j // 2
                                        s.activation(tq[:, j * 512:(j + 1) * 512],
                                                     psq[j][:, :], AF.Relu,
                                                     bias=bq[:, ot:ot + 1])
                                        s.activation(tq2[:, j * 512:(j + 1) * 512],
                                                     psq[j][:, :], AF.Relu, scale=-1.0,
                                                     bias=bqn[:, ot:ot + 1])
                                    elif m == 1:
                                        s.activation(tq[:, j * 512:(j + 1) * 512],
                                                     psq[j][:, :], AF.Relu)
                                        s.activation(tq2[:, j * 512:(j + 1) * 512],
                                                     psq[j][:, :], AF.Relu, scale=-1.0)
                                    else:
                                        tt = half * 4 + j
                                        s.activation(
                                            V_tm[:, tt * D_MODEL + og * 512:
                                                 tt * D_MODEL + og * 512 + 512],
                                            psq[j][:, :], AF.Copy)
                                if m < 2:
                                    for j in range(4):
                                        s.activation(tq2[:, j * 512:(j + 1) * 512],
                                                     tq2[:, j * 512:(j + 1) * 512],
                                                     AF.Exp, scale=-1.0)
                        B("act")
                        if m < 2:
                            with nc.Block() as blk:
                                @blk.vector
                                def _(v):
                                    for j in range(4):
                                        if m == 0:
                                            ot = og * 4 + half * 2 + j // 2
                                            hf = j % 2
                                            dsl = Q_fm[:, ot * T + hf * 512:
                                                       ot * T + hf * 512 + 512]
                                        else:
                                            tt = half * 4 + j
                                            dsl = K_tm[:, tt * D_MODEL + og * 512:
                                                       tt * D_MODEL + og * 512 + 512]
                                        v.tensor_tensor(dsl, tq[:, j * 512:(j + 1) * 512],
                                                        tq2[:, j * 512:(j + 1) * 512],
                                                        ALU.add)
                            B("add")

            for hq in range(4):
                with nc.Block() as blk:
                    @blk.tensor
                    def _(t):
                        mm = None
                        for j in range(4):
                            h = hq * 4 + j
                            for tt in range(8):
                                t.matmul(psq[j][:, 0:128],
                                         K_tm[:, tt * D_MODEL + h * 128:
                                              tt * D_MODEL + h * 128 + 128],
                                         V_tm[:, tt * D_MODEL + h * 128:
                                              tt * D_MODEL + h * 128 + 128],
                                         start=(tt == 0), stop=(tt == 7))
                            for tt in range(8):
                                mm = t.matmul(psq[j][:, 128:129],
                                              K_tm[:, tt * D_MODEL + h * 128:
                                                   tt * D_MODEL + h * 128 + 128],
                                              ctx.C["onescol_b16"][:, :],
                                              start=(tt == 0), stop=(tt == 7))
                        pe.mark(mm, 1)
                B("kv")
                with nc.Block() as blk:
                    @blk.vector
                    def _(v):
                        v.wait_ge(pe.h, pe.v)
                        for j in range(4):
                            h = hq * 4 + j
                            v.tensor_copy(kvp[:, h * 128:(h + 1) * 128],
                                          psq[j][:, 0:128])
                            v.tensor_copy(kvp[:, 2048 + h:2049 + h],
                                          psq[j][:, 128:129])
                B("kve")

        kvb = sbA("L1a_kvb", [128, 2048], BF16)
        ksb = sbA("L1a_ksb", [128, 16], BF16)
        rr = sbA("L1a_rr", [1, T], F32)
        rr2 = sbA("L1a_rr2", [128, T], F32)

        dma([(ar_in, kvp[:, :])], "ari")
        with nc.Block() as blk:
            @blk.gpsimd
            def _(g):
                g.collective_compute(
                    "AllReduce", ALU.add, replica_groups=groups2,
                    ins=[ar_in], outs=[ar_out]).then_inc(cc.h, 1)
                cc.v += 1
                g.wait_ge(cc.h, cc.v)
        B("ar")
        dma([(kvp[:, :], ar_out)], "aro")
        with nc.Block() as blk:
            @blk.vector
            def _(v):
                v.tensor_copy(kvb[:, :], kvp[:, 0:2048])
                v.tensor_copy(ksb[:, :], kvp[:, 2048:2064])
        B("kvc")

        for h in range(16):
            with nc.Block() as blk:
                @blk.tensor
                def _(t):
                    mm = None
                    for hf in range(2):
                        t.matmul(psq[hf][:, :], kvb[:, h * 128:(h + 1) * 128],
                                 Q_fm[:, h * T + hf * 512:h * T + hf * 512 + 512],
                                 start=True, stop=True)
                        mm = t.matmul(
                            psq[2 + hf][0:1, :], ksb[:, h:h + 1],
                            Q_fm[:, h * T + hf * 512:h * T + hf * 512 + 512],
                            start=True, stop=True)
                    pe.mark(mm, 1)
            B("qkv")
            with nc.Block() as blk:
                @blk.scalar
                def _(s):
                    s.wait_ge(pe.h, pe.v)
                    for hf in range(2):
                        s.activation(rr[0:1, hf * 512:hf * 512 + 512],
                                     psq[2 + hf][0:1, :], AF.Ln,
                                     bias=ctx.C["e8"][0:1, 0:1])
                        s.activation(rr[0:1, hf * 512:hf * 512 + 512],
                                     rr[0:1, hf * 512:hf * 512 + 512],
                                     AF.Exp, scale=-1.0)
            B("qr")
            with nc.Block() as blk:
                @blk.tensor
                def _(t):
                    mm = None
                    for hf in range(2):
                        mm = t.matmul(psq[2 + hf][:, :], ctx.C["onesrow_f32"][:, :],
                                      rr[0:1, hf * 512:hf * 512 + 512],
                                      start=True, stop=True)
                    pe.mark(mm, 1)
            B("qrp")
            with nc.Block() as blk:
                @blk.scalar
                def _(s):
                    s.wait_ge(pe.h, pe.v)
                    for hf in range(2):
                        s.activation(rr2[:, hf * 512:hf * 512 + 512],
                                     psq[2 + hf][:, :], AF.Copy)
            B("qre")
            with nc.Block() as blk:
                @blk.vector
                def _(v):
                    for hf in range(2):
                        v.tensor_tensor(
                            at1[:, h * T + hf * 512:h * T + hf * 512 + 512],
                            psq[hf][:, :], rr2[:, hf * 512:hf * 512 + 512],
                            ALU.mult)
            B("qn")

        with contextlib.ExitStack() as sA2:
            sb2 = lambda n, s, d: sA2.enter_context(nc.sbuf_tensor(n, s, d))
            wsb2 = sb2("L1a_w2", [128, 16 * 512], BF16)
            x2a = sb2("L1a_x2a", [128, KCH * T], F32)
            dma([(x2a[:, k * T:(k + 1) * T], x1_buf[k * 128:(k + 1) * 128, :])
                 for k in range(KCH)], "x1l")
            for og in range(4):
                wload(woT, og * 512, wsb2)
                for half in range(2):
                    with nc.Block() as blk:
                        @blk.tensor
                        def _(t):
                            mm = None
                            for oti in range(2):
                                for hf in range(2):
                                    for k in range(KCH):
                                        mm = t.matmul(
                                            psq[oti * 2 + hf][:, :],
                                            wsb2[:, k * 512 + (half * 2 + oti) * 128:
                                                 k * 512 + (half * 2 + oti) * 128 + 128],
                                            at1[:, k * T + hf * 512:
                                                k * T + hf * 512 + 512],
                                            start=(k == 0), stop=(k == KCH - 1))
                            pe.mark(mm, 1)
                    B("wo")
                    with nc.Block() as blk:
                        @blk.vector
                        def _(v):
                            v.wait_ge(pe.h, pe.v)
                            for j in range(4):
                                ot = og * 4 + half * 2 + j // 2
                                hf = j % 2
                                dsl = x2a[:, ot * T + hf * 512:
                                          ot * T + hf * 512 + 512]
                                v.scalar_tensor_tensor(dsl, psq[j][:, :],
                                                       bo[:, ot:ot + 1], dsl,
                                                       ALU.add, ALU.add)
                    B("woa")
            dma([(x2a_buf[k * 128:(k + 1) * 128, :], x2a[:, k * T:(k + 1) * T])
                 for k in range(KCH)], "x2s")

    ksh = ctx.sb("L1f_ksh", [128, 8 * 128], BF16)
    vsh = ctx.sb("L1f_vsh", [128, 8 * 128], BF16)

    with contextlib.ExitStack() as sF1:
        sbF = lambda n, s, d: sF1.enter_context(nc.sbuf_tensor(n, s, d))
        xn2f = sbF("L1f_xn2", [128, KCH * T], BF16)
        wsb = sbF("L1f_w", [128, 16 * 512], BF16)
        kf = sbF("L1f_kf", [128, 8 * D_MODEL], BF16)
        vf = sbF("L1f_vf", [128, 8 * D_MODEL], BF16)
        ind_sb = sbF("L1f_ind", [128, 128], F32)
        top1_sb = sbF("L1f_top1", [128, 8], F32)
        acc2 = sbF("L1f_acc", [128, 2 * 128], F32)

        dma([(xn2f[:, k * T:(k + 1) * T], xn2_buf[k * 128:(k + 1) * 128, :])
             for k in range(KCH)], "xn2f")
        dma([(top1_sb[:, :], top1_buf)], "t1")
        with nc.Block() as blk:
            @blk.gpsimd
            def _(g):
                g.dma_start(
                    out=ind_sb[:, :].rearrange("p (tt g) -> p tt g", g=16),
                    in_=ind_buf.rearrange("(tt p) g -> p tt g", p=128),
                ).then_inc(wl.h, 16)
                wl.v += 16
                g.wait_ge(wl.h, wl.v)
        B("ind")

        for m in range(2):
            dst = (kf, vf)[m]
            for og in range(4):
                wload(wqkvT, D_MODEL + m * D_MODEL + og * 512, wsb)
                for half in range(2):
                    with nc.Block() as blk:
                        @blk.tensor
                        def _(t):
                            mm = None
                            for j in range(4):
                                tt = half * 4 + j
                                for k in range(KCH):
                                    mm = t.matmul(
                                        psq[j][:, :],
                                        xn2f[:, k * T + tt * 128:k * T + tt * 128 + 128],
                                        wsb[:, k * 512:(k + 1) * 512],
                                        start=(k == 0), stop=(k == KCH - 1))
                            pe.mark(mm, 1)
                    B("mm")
                    with nc.Block() as blk:
                        @blk.scalar
                        def _(s):
                            s.wait_ge(pe.h, pe.v)
                            for j in range(4):
                                tt = half * 4 + j
                                s.activation(
                                    dst[:, tt * D_MODEL + og * 512:
                                        tt * D_MODEL + og * 512 + 512],
                                    psq[j][:, :], AF.Copy)
                    B("kv")

        with nc.Block() as blk:
            @blk.vector
            def _(v):
                for tt in range(8):
                    for m in range(2):
                        src = (kf, vf)[m]
                        pp = 0
                        for g in range(16):
                            s_in = src[:, tt * D_MODEL + g * 128:
                                       tt * D_MODEL + g * 128 + 128]
                            sc = ind_sb[:, tt * 16 + g:tt * 16 + g + 1]
                            if g == 0:
                                v.tensor_scalar(acc2[:, 0:128], s_in, sc, None,
                                                ALU.mult)
                                pp = 0
                            else:
                                v.scalar_tensor_tensor(
                                    acc2[:, (1 - pp) * 128:(2 - pp) * 128],
                                    s_in, sc, acc2[:, pp * 128:(pp + 1) * 128],
                                    ALU.mult, ALU.add)
                                pp = 1 - pp
                        if m == 0:
                            v.tensor_scalar(ksh[:, tt * 128:(tt + 1) * 128],
                                            acc2[:, pp * 128:(pp + 1) * 128],
                                            1.0, None, ALU.mult)
                        else:
                            v.tensor_scalar(vsh[:, tt * 128:(tt + 1) * 128],
                                            acc2[:, pp * 128:(pp + 1) * 128],
                                            top1_sb[:, tt:tt + 1], None, ALU.mult)
        B("shared")
        dma([(ag_in[tt * 128:(tt + 1) * 128, 0:128], ksh[:, tt * 128:(tt + 1) * 128])
             for tt in range(8)] +
            [(ag_in[tt * 128:(tt + 1) * 128, 128:256], vsh[:, tt * 128:(tt + 1) * 128])
             for tt in range(8)], "agi")

    with contextlib.ExitStack() as sF2:
        sb2 = lambda n, s, d: sF2.enter_context(nc.sbuf_tensor(n, s, d))
        xn2g = sb2("L1g_xn2", [128, KCH * T], BF16)
        wsb3 = sb2("L1g_w", [128, 16 * 512], BF16)
        qf = sb2("L1g_qf", [128, KCH * T], BF16)
        kg_tm = sb2("L1g_kg", [128, 32 * 128], BF16)
        vg_tm = sb2("L1g_vg", [128, 32 * 128], BF16)
        kT_g = sb2("L1g_kTg", [128, 32 * 128], BF16)
        kT_loc = sb2("L1g_kTl", [128, 8 * 128], BF16)
        x2a2 = sb2("L1g_x2a", [128, KCH * T], F32)
        frow = sb2("L1g_frow", [1, 40], F32)
        biasfull = sb2("L1g_bf", [128, 40], F32)
        P_sb = sb2("L1g_P", [128, 2 * 512], BF16)
        tritmp = sb2("L1g_tri", [128, 2 * 128], F32)
        yterm = sb2("L1g_yt", [128, 512], F32)
        rr = sb2("L1g_rr", [1, 512], F32)
        rr2 = sb2("L1g_rr2", [128, 512], F32)

        with nc.Block() as blk:
            @blk.gpsimd
            def _(g):
                g.collective_compute(
                    "AllGather", ALU.bypass, replica_groups=groups2,
                    ins=[ag_in], outs=[ag_out]).then_inc(cc.h, 1)
                cc.v += 1
                g.wait_ge(cc.h, cc.v)
        B("ag")
        loads = [(frow[0:1, :], flags_in)]
        for gblk in range(8):
            r0 = gather_row_of_block(gblk)
            for j in range(4):
                gt = gblk * 4 + j
                loads.append((kg_tm[:, gt * 128:(gt + 1) * 128],
                              ag_out[r0 + j * 128:r0 + j * 128 + 128, 0:128]))
                loads.append((vg_tm[:, gt * 128:(gt + 1) * 128],
                              ag_out[r0 + j * 128:r0 + j * 128 + 128, 128:256]))
        loads += [(xn2g[:, k * T:(k + 1) * T], xn2_buf[k * 128:(k + 1) * 128, :])
                  for k in range(KCH)]
        loads += [(x2a2[:, k * T:(k + 1) * T], x2a_buf[k * 128:(k + 1) * 128, :])
                  for k in range(KCH)]
        dma(loads, "agl")

        for rnd in range(10):
            with nc.Block() as blk:
                @blk.tensor
                def _(t):
                    mm = None
                    for q in range(4):
                        i = rnd * 4 + q
                        src = (kg_tm[:, i * 128:(i + 1) * 128] if i < 32 else
                               ksh[:, (i - 32) * 128:(i - 31) * 128])
                        mm = t.transpose(ps_tb[:, q * 128:(q + 1) * 128], src,
                                         ctx.C["ident_b16"][:, :])
                    pe.mark(mm, 1)
            B("tr")
            with nc.Block() as blk:
                @blk.vector
                def _(v):
                    v.wait_ge(pe.h, pe.v)
                    for q in range(4):
                        i = rnd * 4 + q
                        dst = (kT_g[:, i * 128:(i + 1) * 128] if i < 32 else
                               kT_loc[:, (i - 32) * 128:(i - 31) * 128])
                        v.tensor_copy(dst, ps_tb[:, q * 128:(q + 1) * 128])
            B("tre")

        for og in range(4):
            wload(wqkvT, og * 512, wsb3)
            for half in range(2):
                with nc.Block() as blk:
                    @blk.tensor
                    def _(t):
                        mm = None
                        for oti in range(2):
                            for hf in range(2):
                                for k in range(KCH):
                                    mm = t.matmul(
                                        psq[oti * 2 + hf][:, :],
                                        wsb3[:, k * 512 + (half * 2 + oti) * 128:
                                             k * 512 + (half * 2 + oti) * 128 + 128],
                                        xn2g[:, k * T + hf * 512:k * T + hf * 512 + 512],
                                        start=(k == 0), stop=(k == KCH - 1))
                        pe.mark(mm, 1)
                B("qmm")
                with nc.Block() as blk:
                    @blk.scalar
                    def _(s):
                        s.wait_ge(pe.h, pe.v)
                        for j in range(4):
                            ot = og * 4 + half * 2 + j // 2
                            hf = j % 2
                            s.activation(qf[:, ot * T + hf * 512:ot * T + hf * 512 + 512],
                                         psq[j][:, :], AF.Copy)
                B("qe")

        with nc.Block() as blk:
            @blk.tensor
            def _(t):
                mm = t.matmul(psq[3][:, 0:40], ctx.C["onesrow_f32"][:, :],
                              frow[0:1, 0:40], start=True, stop=True)
                pe.mark(mm, 1)
        B("bfp")
        with nc.Block() as blk:
            @blk.vector
            def _(v):
                v.wait_ge(pe.h, pe.v)
                v.tensor_scalar(biasfull[:, :], psq[3][:, 0:40], c30[:, 0:1],
                                None, ALU.add)
        B("bfe")

        smm = ctx.sem("L1_smm")
        pex = ctx.sem("L1_pex")
        dtr = ctx.sem("L1_dtr")
        fin = ctx.sem("L1_fin")
        NBULK = (12, 28)

        for h in range(16):
            for g in range(2):
                NB = NBULK[g]
                slots = []
                for s in range(NB):
                    slots.append((kT_g[:, s * 128:(s + 1) * 128],
                                  vg_tm[:, s * 128:(s + 1) * 128],
                                  0, 512, 0, g * 12 + s if g == 0 else 12 + s,
                                  False, s == 0, False))
                for r in range(4):
                    for jb in range(r + 1):
                        lt = g * 4 + jb
                        slots.append((kT_loc[:, lt * 128:(lt + 1) * 128],
                                      vsh[:, lt * 128:(lt + 1) * 128],
                                      r * 128, 128, r * 128, None,
                                      jb == r, False, jb == r))
                NS = len(slots)
                smmb, pexb, dtrb, finb = smm.v, pex.v, dtr.v, fin.v
                tri_ord = {}
                for s in range(NS):
                    if slots[s][6]:
                        tri_ord[s] = len(tri_ord) + 1
                qbase = h * T + g * 512

                with nc.Block() as blk:
                    @blk.tensor
                    def _(t):
                        for s in range(NS + 1):
                            if s < NS:
                                kt, vt, qo, wi, oo, bx, tri, st, sp = slots[s]
                                if s >= 2:
                                    t.wait_ge(pex.h, pexb + s - 1)
                                mm = t.matmul(psq[2 + s % 2][0:128, 0:wi], kt,
                                              qf[:, qbase + qo:qbase + qo + wi],
                                              start=True, stop=True)
                                smm.mark(mm, 1)
                            if s >= 1:
                                kt, vt, qo, wi, oo, bx, tri, st, sp = slots[s - 1]
                                t.wait_ge(pex.h, pexb + s)
                                prhs = P_sb[:, ((s - 1) % 2) * 512:
                                            ((s - 1) % 2) * 512 + wi]
                                t.matmul(psq[0][:, oo:oo + wi], vt, prhs,
                                         start=st, stop=sp)
                                mm = t.matmul(psq[1][0:1, oo:oo + wi],
                                              ctx.C["onescol_b16"][:, :], prhs,
                                              start=st, stop=sp)
                                if s == NS:
                                    fin.mark(mm, 1)
                        t.wait_ge(fin.h, finb + 2)
                        mm = t.matmul(psq[2][:, :], ctx.C["onesrow_f32"][:, :],
                                      rr[0:1, :], start=True, stop=True)
                        fin.mark(mm, 1)

                    @blk.scalar
                    def _(s_):
                        for s in range(NS):
                            kt, vt, qo, wi, oo, bx, tri, st, sp = slots[s]
                            dst = P_sb[:, (s % 2) * 512:(s % 2) * 512 + wi]
                            if tri:
                                s_.wait_ge(dtr.h, dtrb + tri_ord[s])
                                a = s_.activation(
                                    dst, tritmp[:, (s % 2) * 128:(s % 2) * 128 + 128],
                                    AF.Exp, scale=TAU, bias=c30[:, 0:1])
                            else:
                                s_.wait_ge(smm.h, smmb + s + 1)
                                bap = (biasfull[:, bx:bx + 1] if bx is not None
                                       else c30[:, 0:1])
                                a = s_.activation(dst, psq[2 + s % 2][0:128, 0:wi],
                                                  AF.Exp, scale=TAU, bias=bap)
                            pex.mark(a, 1)
                        s_.wait_ge(fin.h, finb + 1)
                        s_.activation(rr[0:1, :], psq[1][0:1, :], AF.Ln)
                        a = s_.activation(rr[0:1, :], rr[0:1, :], AF.Exp, scale=-1.0)
                        fin.mark(a, 1)

                    @blk.vector
                    def _(v):
                        for s in range(NS):
                            if slots[s][6]:
                                v.wait_ge(smm.h, smmb + s + 1)
                                d = v.tensor_tensor(
                                    tritmp[:, (s % 2) * 128:(s % 2) * 128 + 128],
                                    psq[2 + s % 2][0:128, 0:128],
                                    ctx.C["tri_f32"][:, :], ALU.add)
                                dtr.mark(d, 1)
                        v.wait_ge(fin.h, finb + 3)
                        v.tensor_copy(rr2[:, :], psq[2][:, :])
                        v.tensor_tensor(yterm[:, :], psq[0][:, :], rr2[:, :],
                                        ALU.mult)
                        v.tensor_tensor(x2a2[:, qbase:qbase + 512],
                                        x2a2[:, qbase:qbase + 512], yterm[:, :],
                                        ALU.add)
                B("hg")

        dma([(out_buf[k * 128:(k + 1) * 128, :], x2a2[:, k * T:(k + 1) * T])
             for k in range(KCH)], "out")

GROUPS2 = [[0, 1, 2, 3], [4, 5, 6, 7]]


def _mk_nc():
    return bass.Bass("TRN2", target_bir_lowering=False)


def _ctxrun(nc, fn):
    with contextlib.ExitStack() as st:
        ctx = Ctx(nc, st)
        emit_consts(ctx)
        fn(ctx)
    return nc


def _b1():
    nc = _mk_nc()
    dt = lambda n, s, d, k: nc.dram_tensor(n, s, d, kind=k).ap()
    x_t = dt("x_t", [D_MODEL, TH], F32, "ExternalInput")
    xn = dt("xn", [D_MODEL, TH], BF16, "ExternalOutput")
    return _ctxrun(nc, lambda ctx: emit_p1(ctx, x_t, xn))


def _b2():
    nc = _mk_nc()
    dt = lambda n, s, d, k: nc.dram_tensor(n, s, d, kind=k).ap()
    xn = dt("xn", [D_MODEL, TH], BF16, "ExternalInput")
    x_t = dt("x_t", [D_MODEL, TH], F32, "ExternalInput")
    wgT = dt("wgT", [D_MODEL, FFN_I], I32, "ExternalInput")
    wuT = dt("wuT", [D_MODEL, FFN_I], I32, "ExternalInput")
    wdT = dt("wdT", [FFN_I, D_MODEL], I32, "ExternalInput")
    gsc = dt("gsc", [128, 64], F32, "ExternalInput")
    usc = dt("usc", [128, 64], F32, "ExternalInput")
    dsc = dt("dsc", [128, 16], F32, "ExternalInput")
    xffn = dt("xffn", [D_MODEL, T], F32, "ExternalOutput")
    h_buf = nc.dram_tensor("h_buf", [FFN_I, T], BF16, kind="Internal").ap()
    return _ctxrun(nc, lambda ctx: emit_p2(
        ctx, xn, x_t, wgT, wuT, wdT, gsc, usc, dsc, h_buf, xffn))


def _b3():
    nc = _mk_nc()
    dt = lambda n, s, d, k: nc.dram_tensor(n, s, d, kind=k).ap()
    xn = dt("xn", [D_MODEL, TH], BF16, "ExternalInput")
    wipT = dt("wipT", [D_MODEL, PROJ], F32, "ExternalInput")
    convw = dt("convw", [128, 192], F32, "ExternalInput")
    convb = dt("convb", [128, 48], F32, "ExternalInput")
    dtb = dt("dtb", [64, 1], F32, "ExternalInput")
    alog = dt("alog", [64, 1], F32, "ExternalInput")
    gate = dt("gate", [INTER, T], F32, "ExternalOutput")
    hidT = dt("hidT", [T, INTER], BF16, "ExternalOutput")
    Bb = dt("Bb", [1024, T], BF16, "ExternalOutput")
    Cb = dt("Cb", [1024, T], BF16, "ExternalOutput")
    acum = dt("acum", [64, T], F32, "ExternalOutput")
    nacT = dt("nacT", [128, 512], F32, "ExternalOutput")
    dt2T = dt("dt2T", [128, 512], F32, "ExternalOutput")
    UD = dt("UD", [128, 512], F32, "ExternalOutput")
    return _ctxrun(nc, lambda ctx: emit_p3(
        ctx, xn, wipT, convw, convb, dtb, alog,
        gate, hidT, Bb, Cb, acum, nacT, dt2T, UD))


def _b4():
    nc = _mk_nc()
    dt = lambda n, s, d, k: nc.dram_tensor(n, s, d, kind=k).ap()
    gate = dt("gate", [INTER, T], F32, "ExternalInput")
    hidT = dt("hidT", [T, INTER], BF16, "ExternalInput")
    Bb = dt("Bb", [1024, T], BF16, "ExternalInput")
    Cb = dt("Cb", [1024, T], BF16, "ExternalInput")
    acum = dt("acum", [64, T], F32, "ExternalInput")
    nacT = dt("nacT", [128, 512], F32, "ExternalInput")
    dt2T = dt("dt2T", [128, 512], F32, "ExternalInput")
    UD = dt("UD", [128, 512], F32, "ExternalInput")
    Drow = dt("Drow", [1, 64], F32, "ExternalInput")
    yg = dt("yg", [INTER, T], BF16, "ExternalOutput")
    var = dt("var", [1, T], F32, "ExternalOutput")
    return _ctxrun(nc, lambda ctx: emit_p4(
        ctx, gate, hidT, Bb, Cb, acum, nacT, dt2T, UD, Drow, yg, var))


def _b5():
    nc = _mk_nc()
    dt = lambda n, s, d, k: nc.dram_tensor(n, s, d, kind=k).ap()
    yg = dt("yg", [INTER, T], BF16, "ExternalInput")
    var = dt("var", [1, T], F32, "ExternalInput")
    wopT = dt("wopT", [INTER, D_MODEL], F32, "ExternalInput")
    normw = dt("normw", [128, 32], F32, "ExternalInput")
    xffn = dt("xffn", [D_MODEL, T], F32, "ExternalInput")
    wgT = dt("wgT", [D_MODEL, 16], F32, "ExternalInput")
    x1 = dt("x1", [D_MODEL, T], F32, "ExternalOutput")
    xn2 = dt("xn2", [D_MODEL, T], BF16, "ExternalOutput")
    ind = dt("ind", [T, 16], BF16, "ExternalOutput")
    top1 = dt("top1", [128, 8], F32, "ExternalOutput")
    return _ctxrun(nc, lambda ctx: emit_p5(
        ctx, yg, var, wopT, normw, xffn, wgT, x1, xn2, ind, top1))


def _b6():
    nc = _mk_nc()
    dt = lambda n, s, d, k: nc.dram_tensor(n, s, d, kind=k).ap()
    xn2 = dt("xn2", [D_MODEL, T], BF16, "ExternalInput")
    x1 = dt("x1", [D_MODEL, T], F32, "ExternalInput")
    wq = dt("wqT", [D_MODEL, D_MODEL], F32, "ExternalInput")
    wk = dt("wkT", [D_MODEL, D_MODEL], F32, "ExternalInput")
    wv = dt("wvT", [D_MODEL, D_MODEL], F32, "ExternalInput")
    wo = dt("woT", [D_MODEL, D_MODEL], F32, "ExternalInput")
    bqc = dt("bq_col", [128, 16], F32, "ExternalInput")
    boc = dt("bo_col", [128, 16], F32, "ExternalInput")
    brow = dt("brow", [1, 2 * D_MODEL], F32, "ExternalInput")
    wqkv = dt("wqkvT", [D_MODEL, 3 * D_MODEL], F32, "ExternalInput")
    ind = dt("ind", [T, 16], BF16, "ExternalInput")
    top1 = dt("top1", [128, 8], F32, "ExternalInput")
    flags = dt("flags", [1, 40], F32, "ExternalInput")
    out = dt("out", [D_MODEL, T], F32, "ExternalOutput")
    ar_in = nc.dram_tensor("ar_in", [128, 2064], F32).ap()
    ar_out = nc.dram_tensor("ar_out", [128, 2064], F32, addr_space="Local").ap()
    ag_in = nc.dram_tensor("ag_in", [1024, 256], BF16).ap()
    ag_out = nc.dram_tensor("ag_out", [4096, 256], BF16).ap()
    x2ab = nc.dram_tensor("x2ab", [D_MODEL, T], F32).ap()
    return _ctxrun(nc, lambda ctx: emit_L1(
        ctx, xn2, x1, wq, wk, wv, wo, bqc, boc, brow, wqkv, ind, top1, flags,
        ar_in, ar_out, ag_in, ag_out, x2ab, out, GROUPS2))


def _kernel_multi(**inputs):
    sh, per = host_prep(inputs)
    cores = list(range(NCORE))
    brow = np.concatenate([sh["bk_row"], sh["bv_row"]], axis=1)

    r = run_bass_kernel_spmd(
        _b1(), [dict(x_t=per[c]["x_t"]) for c in cores], core_ids=cores)
    xn = [np.asarray(r.results[c]["xn"]) for c in cores]

    r = run_bass_kernel_spmd(
        _b2(), [dict(xn=xn[c], x_t=per[c]["x_t"], wgT=sh["wgT"], wuT=sh["wuT"],
                     wdT=sh["wdT"], gsc=sh["gate_scale"], usc=sh["up_scale"],
                     dsc=sh["down_scale"]) for c in cores], core_ids=cores)
    xffn = [np.asarray(r.results[c]["xffn"]) for c in cores]

    r = run_bass_kernel_spmd(
        _b3(), [dict(xn=xn[c], wipT=sh["w_inprojT"], convw=sh["conv_w"],
                     convb=sh["conv_b"], dtb=sh["dt_bias"], alog=sh["A_log"])
                for c in cores], core_ids=cores)
    p3 = [{k: np.asarray(r.results[c][k]) for k in
           ("gate", "hidT", "Bb", "Cb", "acum", "nacT", "dt2T", "UD")}
          for c in cores]

    r = run_bass_kernel_spmd(
        _b4(), [dict(p3[c], Drow=sh["D_row"]) for c in cores], core_ids=cores)
    p4 = [{k: np.asarray(r.results[c][k]) for k in ("yg", "var")} for c in cores]

    r = run_bass_kernel_spmd(
        _b5(), [dict(yg=p4[c]["yg"], var=p4[c]["var"], wopT=sh["w_outprojT"],
                     normw=sh["norm_w"], xffn=xffn[c], wgT=sh["wgateT"])
                for c in cores], core_ids=cores)
    p5 = [{k: np.asarray(r.results[c][k]) for k in ("x1", "xn2", "ind", "top1")}
          for c in cores]

    r = run_bass_kernel_spmd(
        _b6(), [dict(xn2=p5[c]["xn2"], x1=p5[c]["x1"], wqT=sh["wqT"],
                     wkT=sh["wkT"], wvT=sh["wvT"], woT=sh["woT"],
                     bq_col=sh["bq_col"], bo_col=sh["bo_col"], brow=brow,
                     wqkvT=sh["wqkvT"], ind=p5[c]["ind"], top1=p5[c]["top1"],
                     flags=per[c]["flags"]) for c in cores], core_ids=cores)

    out = np.empty((2, 4096, D_MODEL), np.float32)
    for c in cores:
        b, cp = c // 4, c % 4
        ob = np.asarray(r.results[c]["out"]).astype(np.float32)
        out[b, cp * 512:(cp + 1) * 512, :] = ob[:, 0:512].T
        out[b, (7 - cp) * 512:(7 - cp) * 512 + 512, :] = ob[:, 512:1024].T
    return out


def _bF():
    nc = _mk_nc()
    dt = lambda n, s, d, k: nc.dram_tensor(n, s, d, kind=k).ap()
    di = lambda n, s, d: nc.dram_tensor(n, s, d, kind="Internal").ap()
    x_t = dt("x_t", [D_MODEL, TH], F32, "ExternalInput")
    wgT = dt("wgT", [D_MODEL, FFN_I], I32, "ExternalInput")
    wuT = dt("wuT", [D_MODEL, FFN_I], I32, "ExternalInput")
    wdT = dt("wdT", [FFN_I, D_MODEL], I32, "ExternalInput")
    gsc = dt("gsc", [128, 64], F32, "ExternalInput")
    usc = dt("usc", [128, 64], F32, "ExternalInput")
    dsc = dt("dsc", [128, 16], F32, "ExternalInput")
    wipT = dt("wipT", [D_MODEL, PROJ], F32, "ExternalInput")
    convw = dt("convw", [128, 192], F32, "ExternalInput")
    convb = dt("convb", [128, 48], F32, "ExternalInput")
    dtb = dt("dtb", [64, 1], F32, "ExternalInput")
    alog = dt("alog", [64, 1], F32, "ExternalInput")
    Drow = dt("Drow", [1, 64], F32, "ExternalInput")
    wopT = dt("wopT", [INTER, D_MODEL], F32, "ExternalInput")
    normw = dt("normw", [128, 32], F32, "ExternalInput")
    wgateT = dt("wgateT", [D_MODEL, 16], F32, "ExternalInput")
    wq = dt("wqT", [D_MODEL, D_MODEL], F32, "ExternalInput")
    wk = dt("wkT", [D_MODEL, D_MODEL], F32, "ExternalInput")
    wv = dt("wvT", [D_MODEL, D_MODEL], F32, "ExternalInput")
    wo = dt("woT", [D_MODEL, D_MODEL], F32, "ExternalInput")
    bqc = dt("bq_col", [128, 16], F32, "ExternalInput")
    boc = dt("bo_col", [128, 16], F32, "ExternalInput")
    brow = dt("brow", [1, 2 * D_MODEL], F32, "ExternalInput")
    wqkv = dt("wqkvT", [D_MODEL, 3 * D_MODEL], F32, "ExternalInput")
    flags = dt("flags", [1, 40], F32, "ExternalInput")
    out = dt("out", [D_MODEL, T], F32, "ExternalOutput")

    xn = di("xn", [D_MODEL, TH], BF16)
    xffn = di("xffn", [D_MODEL, T], F32)
    h_buf = di("h_buf", [FFN_I, T], BF16)
    gate = di("gate", [INTER, T], F32)
    hidT = di("hidT", [T, INTER], BF16)
    Bb = di("Bb", [1024, T], BF16)
    Cb = di("Cb", [1024, T], BF16)
    acum = di("acum", [64, T], F32)
    nacT = di("nacT", [128, 512], F32)
    dt2T = di("dt2T", [128, 512], F32)
    UD = di("UD", [128, 512], F32)
    yg = di("yg", [INTER, T], BF16)
    var = di("var", [1, T], F32)
    x1 = di("x1", [D_MODEL, T], F32)
    xn2 = di("xn2", [D_MODEL, T], BF16)
    ind = di("ind", [T, 16], BF16)
    top1 = di("top1", [128, 8], F32)
    ar_in = nc.dram_tensor("ar_in", [128, 2064], F32).ap()
    ar_out = nc.dram_tensor("ar_out", [128, 2064], F32, addr_space="Local").ap()
    ag_in = nc.dram_tensor("ag_in", [1024, 256], BF16).ap()
    ag_out = nc.dram_tensor("ag_out", [4096, 256], BF16).ap()
    x2ab = nc.dram_tensor("x2ab", [D_MODEL, T], F32).ap()

    with contextlib.ExitStack() as st:
        ctx = Ctx(nc, st)
        emit_consts(ctx)
        emit_p1(ctx, x_t, xn)
        barrier(ctx, "ph")
        emit_p2(ctx, xn, x_t, wgT, wuT, wdT, gsc, usc, dsc, h_buf, xffn)
        barrier(ctx, "ph")
        emit_p3(ctx, xn, wipT, convw, convb, dtb, alog,
                gate, hidT, Bb, Cb, acum, nacT, dt2T, UD)
        barrier(ctx, "ph")
        emit_p4(ctx, gate, hidT, Bb, Cb, acum, nacT, dt2T, UD, Drow, yg, var)
        barrier(ctx, "ph")
        emit_p5(ctx, yg, var, wopT, normw, xffn, wgateT, x1, xn2, ind, top1)
        barrier(ctx, "ph")
        emit_L1(ctx, xn2, x1, wq, wk, wv, wo, bqc, boc, brow, wqkv, ind, top1,
                flags, ar_in, ar_out, ag_in, ag_out, x2ab, out, GROUPS2)
    return nc


def kernel(**inputs):
    sh, per = host_prep(inputs)
    cores = list(range(NCORE))
    brow = np.concatenate([sh["bk_row"], sh["bv_row"]], axis=1)
    ins = [dict(x_t=per[c]["x_t"], flags=per[c]["flags"],
                wgT=sh["wgT"], wuT=sh["wuT"], wdT=sh["wdT"],
                gsc=sh["gate_scale"], usc=sh["up_scale"], dsc=sh["down_scale"],
                wipT=sh["w_inprojT"], convw=sh["conv_w"], convb=sh["conv_b"],
                dtb=sh["dt_bias"], alog=sh["A_log"], Drow=sh["D_row"],
                wopT=sh["w_outprojT"], normw=sh["norm_w"], wgateT=sh["wgateT"],
                wqT=sh["wqT"], wkT=sh["wkT"], wvT=sh["wvT"], woT=sh["woT"],
                bq_col=sh["bq_col"], bo_col=sh["bo_col"], brow=brow,
                wqkvT=sh["wqkvT"]) for c in cores]
    r = run_bass_kernel_spmd(_bF(), ins, core_ids=cores)
    out = np.empty((2, 4096, D_MODEL), np.float32)
    for c in cores:
        b, cp = c // 4, c % 4
        ob = np.asarray(r.results[c]["out"]).astype(np.float32)
        out[b, cp * 512:(cp + 1) * 512, :] = ob[:, 0:512].T
        out[b, (7 - cp) * 512:(7 - cp) * 512 + 512, :] = ob[:, 512:1024].T
    return out
